# version 05
# baseline (speedup 1.0000x reference)
import contextlib
import numpy as np
import concourse.bass as bass
import concourse.mybir as mybir
from concourse.bass_utils import run_bass_kernel_spmd

F32 = mybir.dt.float32
BF16 = mybir.dt.bfloat16
AF = mybir.ActivationFunctionType
ALU = mybir.AluOpType
AX = mybir.AxisListType

D = 2048
TKV = 4096
TQ = 2048
NH = 16
NKC = 16
EPS = 1e-5
LAM_INIT = 0.2
EXP_SHIFT = -12.0
NE = 8
NFC_E = 11
NFC_ALL = 110
N_ELOOP = 10

BUILD_PHASES = 5
DEBUG_SCRATCH = False

ENGINES = ("sync", "gpsimd", "scalar", "vector", "tensor")


class Prog:

    def __init__(self, nc, stack):
        self.nc = nc
        self.ops = []
        self.cnt = {}
        self.sems = {}
        self.pending_gate = {e: [] for e in ENGINES}
        self._stack = stack

    def sem(self, name):
        if name not in self.sems:
            self.sems[name] = self._stack.enter_context(self.nc.semaphore(name))
            self.cnt[name] = 0
        return name

    def n(self, name):
        return self.cnt.get(name, 0)

    def op(self, engine, fn, waits=(), incs=()):
        w = list(self.pending_gate[engine]) + [(s, v) for (s, v) in waits if v > 0]
        self.pending_gate[engine] = []
        inc_list = []
        for (s, k) in incs:
            self.sem(s)
            self.cnt[s] += k
            inc_list.append((s, k))
        for (s, _) in w:
            self.sem(s)
        self.ops.append((engine, fn, w, inc_list))

    def gate(self, engines=ENGINES, sems=None):
        sems = sems if sems is not None else list(self.cnt.keys())
        for e in engines:
            for s in sems:
                v = self.cnt.get(s, 0)
                if v > 0:
                    self.pending_gate[e].append((s, v))

    def emit(self, blk):
        prog = self
        for ename in ENGINES:
            def make(ename):
                def section(eng):
                    waited = {}
                    for (en, fn, waits, incs) in prog.ops:
                        if en != ename:
                            continue
                        for (s, v) in waits:
                            if v > waited.get(s, 0):
                                eng.wait_ge(prog.sems[s], v)
                                waited[s] = v
                        inst = fn(eng)
                        for (s, k) in incs:
                            inst.then_inc(prog.sems[s], k)
                return section
            getattr(blk, ename)(make(ename))


def _mmgroup(ps, pairs):
    def fn(t):
        last = None
        for i, (lhs, rhs) in enumerate(pairs):
            last = t.matmul(ps, lhs, rhs, start=(i == 0), stop=(i == len(pairs) - 1))
        return last
    return fn


def build_nc():
    nc = bass.Bass("TRN2", target_bir_lowering=False, debug=False)
    io_kind = "ExternalOutput" if DEBUG_SCRATCH else "Internal"

    x_in = nc.dram_tensor("x_in", [TKV, D], F32, kind="ExternalInput").ap()
    wattn = nc.dram_tensor("wattn", [48, 128, 2048], F32, kind="ExternalInput").ap()
    wproj = nc.dram_tensor("wproj", [16, 128, 2048], F32, kind="ExternalInput").ap()
    lam_in = nc.dram_tensor("lam_in", [128, 64], F32, kind="ExternalInput").ap()
    gw_in = nc.dram_tensor("gw_in", [16, 128, 8], F32, kind="ExternalInput").ap()
    wgu = nc.dram_tensor("wgu", [NE, 22, 128, 2048], F32, kind="ExternalInput").ap()
    wsgu = nc.dram_tensor("wsgu", [44, 128, 2048], F32, kind="ExternalInput").ap()
    wall_in = nc.dram_tensor("wall_in", [16, NFC_ALL, 128, 128], F32, kind="ExternalInput").ap()

    out_d = nc.dram_tensor("out", [TQ, D], F32, kind="ExternalOutput").ap()

    qkvT_s = nc.dram_tensor("qkvT_s", [48, 128, TKV], BF16, kind=io_kind).ap()
    v_s2 = nc.dram_tensor("v_s2", [16, 32, 128, 128], BF16, kind=io_kind).ap()
    xmid_s = nc.dram_tensor("xmid_s", [TQ, D], F32, kind=io_kind).ap()
    ln2t_s = nc.dram_tensor("ln2t_s", [16, 128, TQ], BF16, kind=io_kind).ap()
    act_s = nc.dram_tensor("act_s", [4, 128, NFC_ALL, 512], BF16, kind=io_kind).ap()
    wallbf_s = nc.dram_tensor("wallbf_s", [16, NFC_ALL, 128, 128], BF16, kind="Internal").ap()

    st = contextlib.ExitStack()
    E = st.enter_context
    with st:
        ident_bf = E(nc.sbuf_tensor("ident_bf", [128, 128], BF16))
        ident_f32 = E(nc.sbuf_tensor("ident_f32", [128, 128], F32))
        ones_bf = E(nc.sbuf_tensor("ones_bf", [128, 128], BF16))
        lam_sb = E(nc.sbuf_tensor("lam_sb", [128, 64], F32))
        prod2 = E(nc.sbuf_tensor("prod2", [128, 32], F32))
        rowsum2f = E(nc.sbuf_tensor("rowsum2f", [128, 2], F32))
        rowsum2 = E(nc.sbuf_tensor("rowsum2", [128, 2], BF16))
        expd = E(nc.sbuf_tensor("expd", [1, 2], F32))
        lam_val = E(nc.sbuf_tensor("lam_val", [1, 1], F32))
        lam_val_bf = E(nc.sbuf_tensor("lam_val_bf", [1, 1], BF16))
        lamq2_col = E(nc.sbuf_tensor("lamq2_col", [128, 1], F32))
        gw_sb = E(nc.sbuf_tensor("gw_sb", [128, 128], BF16))
        sq_scrap = E(nc.sbuf_tensor("sq_scrap", [128, 2048], F32))
        eps_col = E(nc.sbuf_tensor("eps_col", [128, 1], F32))
        shift_col = E(nc.sbuf_tensor("shift_col", [128, 1], F32))
        COLS = {}
        for nm in ("sum", "ssq", "mean", "m2", "var", "rstd", "nmr"):
            for grp in ("a", "b"):
                COLS[(grp, nm)] = [E(nc.sbuf_tensor(f"c_{grp}{nm}{i}", [128, 1], F32))
                                   for i in range(2)]

        P = Prog(nc, st)
        for s in ("dX", "dW", "dS", "dO", "tK", "vK", "aK", "gK", "vG", "aG"):
            P.sem(s)

        blk = E(nc.Block())

        def ln_stats(grp, par, x_ap, n_free, guard_extra=()):
            c = lambda nm: COLS[(grp, nm)][par]
            P.op("scalar",
                 lambda s, x=x_ap, o=c("ssq"): s.activation(
                     sq_scrap[:, 0:n_free], x, AF.Square, accum_out=o[:]),
                 waits=guard_extra, incs=[("aK", 1)])
            a_ssq = P.n("aK")
            P.op("vector",
                 lambda v, x=x_ap, o=c("sum"): v.tensor_reduce(o[:], x, AX.X, ALU.add),
                 waits=guard_extra, incs=[("vG", 1)])
            P.op("vector",
                 lambda v, i=c("sum"), o=c("mean"): v.tensor_scalar_mul(o[:], i[:], 1.0 / n_free),
                 waits=[("vG", P.n("vG"))], incs=[("vG", 1)])
            P.op("vector",
                 lambda v, i=c("mean"), o=c("m2"): v.tensor_tensor(o[:], i[:], i[:], ALU.mult),
                 waits=[("vG", P.n("vG"))], incs=[("vG", 1)])
            P.op("vector",
                 lambda v, i=c("ssq"), m=c("m2"), o=c("var"):
                     v.scalar_tensor_tensor(o[:], i[:], 1.0 / n_free, m[:], ALU.mult, ALU.subtract),
                 waits=[("vG", P.n("vG")), ("aK", a_ssq)], incs=[("vK", 1)])
            v_var = P.n("vK")
            P.op("scalar",
                 lambda s, i=c("var"), o=c("rstd"): s.activation(o[:], i[:], AF.Ln, bias=eps_col[:]),
                 waits=[("vK", v_var)], incs=[("aG", 1)])
            P.op("scalar",
                 lambda s, io=c("rstd"): s.activation(io[:], io[:], AF.Exp, scale=-0.5),
                 waits=[("aG", P.n("aG"))], incs=[("aK", 1)])
            a_rstd = P.n("aK")
            P.op("vector",
                 lambda v, m=c("mean"), r=c("rstd"), o=c("nmr"):
                     v.scalar_tensor_tensor(o[:], m[:], -1.0, r[:], ALU.mult, ALU.mult),
                 waits=[("aK", a_rstd)], incs=[("vK", 1)])
            return {"aK_sq": a_ssq, "aK_rstd": a_rstd, "vK_nmr": P.n("vK"),
                    "mean": c("mean"), "rstd": c("rstd"), "nmr": c("nmr")}

        P.op("gpsimd", lambda g: g.memset(ident_bf[:], 0.0), incs=[("gK", 1)])
        P.op("gpsimd", lambda g: g.affine_select(out=ident_bf[:], in_=ident_bf[:],
                                                 compare_op=ALU.not_equal, fill=1.0, base=0,
                                                 pattern=[[-1, 128]], channel_multiplier=1),
             waits=[("gK", P.n("gK"))])
        P.op("gpsimd", lambda g: g.memset(ident_f32[:], 0.0), incs=[("gK", 1)])
        P.op("gpsimd", lambda g: g.affine_select(out=ident_f32[:], in_=ident_f32[:],
                                                 compare_op=ALU.not_equal, fill=1.0, base=0,
                                                 pattern=[[-1, 128]], channel_multiplier=1),
             waits=[("gK", P.n("gK"))])
        P.op("gpsimd", lambda g: g.memset(eps_col[:], EPS))
        P.op("gpsimd", lambda g: g.memset(shift_col[:], EXP_SHIFT))
        P.op("gpsimd", lambda g: g.memset(ones_bf[:], 1.0), incs=[("gK", 1)])
        g_const = P.n("gK")
        P.op("gpsimd", lambda g: g.dma_start(lam_sb[:], lam_in), incs=[("dW", 16)])
        P.op("gpsimd", lambda g: g.dma_start(gw_sb[:].rearrange("p (a e) -> p a e", a=16),
                                             gw_in.rearrange("a p e -> p a e")), incs=[("dW", 16)])
        d_lamgw = P.n("dW")

        ph0_ps = contextlib.ExitStack()
        lam_ps = ph0_ps.enter_context(nc.psum_tensor("lam_ps", [1, 2], F32))
        lam_bc_ps = ph0_ps.enter_context(nc.psum_tensor("lam_bc_ps", [128, 1], F32))

        P.op("vector", lambda v: v.tensor_tensor(prod2[:, 0:16], lam_sb[:, 0:16], lam_sb[:, 16:32], ALU.mult),
             waits=[("dW", d_lamgw)])
        P.op("vector", lambda v: v.tensor_tensor(prod2[:, 16:32], lam_sb[:, 32:48], lam_sb[:, 48:64], ALU.mult),
             incs=[("vG", 1)])
        P.op("vector", lambda v: v.tensor_reduce(rowsum2f[:, 0:1], prod2[:, 0:16], AX.X, ALU.add),
             waits=[("vG", P.n("vG"))])
        P.op("vector", lambda v: v.tensor_reduce(rowsum2f[:, 1:2], prod2[:, 16:32], AX.X, ALU.add),
             incs=[("vG", 1)])
        P.op("vector", lambda v: v.tensor_copy(rowsum2[:], rowsum2f[:]),
             waits=[("vG", P.n("vG"))], incs=[("vK", 1)])
        P.op("tensor", lambda t: t.matmul(lam_ps[:], ones_bf[:, 0:1], rowsum2[:], start=True, stop=True),
             waits=[("gK", g_const), ("vK", P.n("vK"))], incs=[("tK", 1)])
        P.op("scalar", lambda s: s.activation(expd[:], lam_ps[:], AF.Exp),
             waits=[("tK", P.n("tK"))], incs=[("aK", 1)])
        P.op("vector", lambda v: v.tensor_tensor(lam_val[:], expd[:, 0:1], expd[:, 1:2], ALU.subtract),
             waits=[("aK", P.n("aK"))], incs=[("vG", 1)])
        P.op("vector", lambda v: v.tensor_scalar_add(lam_val[:], lam_val[:], LAM_INIT),
             waits=[("vG", P.n("vG"))], incs=[("vG", 1)])
        P.op("vector", lambda v: v.tensor_copy(lam_val_bf[:], lam_val[:]),
             waits=[("vG", P.n("vG"))], incs=[("vK", 1)])
        P.op("tensor", lambda t: t.matmul(lam_bc_ps[:], ones_bf[0:1, :], lam_val_bf[:], start=True, stop=True),
             waits=[("vK", P.n("vK"))], incs=[("tK", 1)])
        P.op("vector", lambda v: v.tensor_scalar_mul(lamq2_col[:], lam_bc_ps[:], -0.125),
             waits=[("tK", P.n("tK"))], incs=[("vK", 1)])
        v_lamq2 = P.n("vK")
        ph0_ps.close()

        with contextlib.ExitStack() as ph1:
            E1 = ph1.enter_context
            lnT_full = E1(nc.sbuf_tensor("lnT_full", [128, NKC * TKV], BF16))
            xa = [E1(nc.sbuf_tensor(f"xa{i}", [128, D], F32)) for i in range(2)]
            ln_tm = [E1(nc.sbuf_tensor(f"ln_tm{i}", [128, D], BF16)) for i in range(2)]
            wslab = [E1(nc.sbuf_tensor(f"wslab{i}", [128, 2048], BF16)) for i in range(2)]
            qt_sb = [E1(nc.sbuf_tensor(f"qt_sb{i}", [128, 512], BF16)) for i in range(4)]
            vtmp = [E1(nc.sbuf_tensor(f"vtmp{i}", [128, 512], BF16)) for i in range(2)]
            vt_sb = [E1(nc.sbuf_tensor(f"vt_sb{i}", [128, 512], BF16)) for i in range(2)]
            ps_qkv = [E1(nc.psum_tensor(f"ps_qkv{i}", [128, 1024], F32)) for i in range(2)]
            ps_tr = [E1(nc.psum_tensor(f"ps_tr{i}", [128, 1024], BF16)) for i in range(2)]

            NT_A = TKV // 128
            lnT_r = lnT_full[:].rearrange("p (k t) -> p k t", k=NKC)

            apply_cnt, tr_cnt, ev_cnt, load_cnt = [], [], [], []
            for i in range(NT_A):
                par = i % 2
                w = [("aK", apply_cnt[i - 2])] if i >= 2 else []
                P.op("sync", lambda sy, i=i, par=par: sy.dma_start(
                    xa[par][:], x_in[i * 128:(i + 1) * 128, :]), waits=w, incs=[("dX", 16)])
                load_cnt.append(P.n("dX"))
                st_i = ln_stats("a", par, xa[par][:], D, guard_extra=[("dX", load_cnt[i])])
                w = [("vK", st_i["vK_nmr"])]
                if i >= 2:
                    w.append(("tK", tr_cnt[2 * (i - 2) + 1]))
                P.op("scalar",
                     lambda s, par=par, stv=st_i: s.activation(
                         ln_tm[par][:], xa[par][:], AF.Identity, bias=stv["nmr"][:], scale=stv["rstd"][:]),
                     waits=w, incs=[("aK", 1)])
                apply_cnt.append(P.n("aK"))
                for rnd in range(2):
                    w = [("aK", apply_cnt[i]), ("gK", g_const)]
                    if i == 0:
                        w.append(("vK", v_lamq2))
                    prev = 2 * i + rnd - 2
                    if prev >= 0:
                        w.append(("vK", ev_cnt[prev]))
                    def trfn(t, par=par, rnd=rnd):
                        last = None
                        for k in range(8):
                            kc = rnd * 8 + k
                            last = t.transpose(ps_tr[rnd][:, k * 128:(k + 1) * 128],
                                               ln_tm[par][:, kc * 128:(kc + 1) * 128], ident_bf[:])
                        return last
                    P.op("tensor", trfn, waits=w, incs=[("tK", 1)])
                    tr_cnt.append(P.n("tK"))
                    P.op("vector",
                         lambda v, i=i, rnd=rnd: v.tensor_copy(
                             lnT_r[:, rnd * 8:(rnd + 1) * 8, i * 128:(i + 1) * 128],
                             ps_tr[rnd][:].rearrange("p (k f) -> p k f", k=8)),
                         waits=[("tK", tr_cnt[-1])], incs=[("vK", 1)])
                    ev_cnt.append(P.n("vK"))
            lnT_done = P.n("vK")

            mb_groups = [4 if (mb < 8 or 16 <= mb < 24) else 8 for mb in range(48)]
            seq = [(mb, gi) for mb in range(48) for gi in range(mb_groups[mb])]

            slab_cnt = {}
            grp_cnt = []
            qk_ev_cnt, qk_st_cnt = [], []
            v_ev_cnt, v_tr_cnt, v_te_cnt, v_st_cnt = [], [], [], []
            mb_last_grp = {}
            qk_idx_of_group = {}
            v_idx_of_group = {}

            for mb in range(2):
                P.op("gpsimd", lambda g, mb=mb: g.dma_start(wslab[mb % 2][:], wattn[mb]),
                     incs=[("dW", 16)])
                slab_cnt[mb] = P.n("dW")

            nqk = nv = 0
            pend_v = []

            def flush_vtr():
                while pend_v:
                    pnv, pmb, pgi = pend_v.pop(0)
                    w = [("vK", v_ev_cnt[pnv])]
                    if pnv >= 2:
                        w.append(("vK", v_te_cnt[pnv - 2]))
                    def vtr(t, b=vtmp[pnv % 2], pt=ps_tr[pnv % 2]):
                        last = None
                        for tb in range(4):
                            last = t.transpose(pt[:, tb * 128:(tb + 1) * 128],
                                               b[:, tb * 128:(tb + 1) * 128], ident_bf[:])
                        return last
                    P.op("tensor", vtr, waits=w, incs=[("tK", 1)])
                    v_tr_cnt.append(P.n("tK"))
                    w = [("tK", v_tr_cnt[pnv])]
                    if pnv >= 2:
                        w.append(("dS", v_st_cnt[pnv - 2]))
                    P.op("vector",
                         lambda v, b=vt_sb[pnv % 2], pt=ps_tr[pnv % 2]:
                             v.tensor_copy(b[:].rearrange("p (tb f) -> p tb f", tb=4),
                                           pt[:, 0:512].rearrange("p (tb f) -> p tb f", tb=4)),
                         waits=w, incs=[("vK", 1)])
                    v_te_cnt.append(P.n("vK"))
                    P.op("scalar",
                         lambda s, vb=pmb - 32, gi=pgi, b=vt_sb[pnv % 2]:
                             s.dma_start(v_s2[vb, gi * 4:(gi + 1) * 4, :, :].rearrange("tb p f -> p tb f"),
                                         b[:].rearrange("p (tb f) -> p tb f", tb=4)),
                         waits=[("vK", v_te_cnt[pnv])], incs=[("dS", 16)])
                    v_st_cnt.append(P.n("dS"))

            for j, (mb, gi) in enumerate(seq):
                par = (j % 4) // 2
                half = j % 2
                ps = ps_qkv[par][:, half * 512:half * 512 + 512]
                w = [("dW", slab_cnt[mb])]
                if j == 0:
                    w += [("vK", max(lnT_done, v_lamq2))]
                if j >= 4:
                    pm, _ = seq[j - 4]
                    if pm < 32:
                        w.append(("aK", qk_ev_cnt[qk_idx_of_group[j - 4]]))
                    else:
                        w.append(("vK", v_ev_cnt[v_idx_of_group[j - 4]]))
                pairs = [(wslab[mb % 2][:, k * 128:(k + 1) * 128],
                          lnT_full[:, k * TKV + gi * 512: k * TKV + gi * 512 + 512])
                         for k in range(NKC)]
                P.op("tensor", _mmgroup(ps, pairs), waits=w, incs=[("tK", 1)])
                grp_cnt.append(P.n("tK"))
                flush_vtr()
                if gi == mb_groups[mb] - 1:
                    mb_last_grp[mb] = P.n("tK")
                    if mb + 2 < 48:
                        P.op("gpsimd", lambda g, mb2=mb + 2: g.dma_start(wslab[mb2 % 2][:], wattn[mb2]),
                             waits=[("tK", mb_last_grp[mb])], incs=[("dW", 16)])
                        slab_cnt[mb + 2] = P.n("dW")

                if mb < 32:
                    qk_idx_of_group[j] = nqk
                    w = [("tK", grp_cnt[j])]
                    if nqk >= 4:
                        w.append(("dS", qk_st_cnt[nqk - 4]))
                    if mb < 8:
                        fn = (lambda s, ps=ps, b=qt_sb[nqk % 4]:
                              s.activation(b[:], ps, AF.Copy, scale=0.125))
                    elif 16 <= mb < 24:
                        fn = (lambda s, ps=ps, b=qt_sb[nqk % 4]:
                              s.activation(b[:], ps, AF.Identity, scale=lamq2_col[:]))
                    else:
                        fn = (lambda s, ps=ps, b=qt_sb[nqk % 4]:
                              s.activation(b[:], ps, AF.Copy))
                    P.op("scalar", fn, waits=w, incs=[("aK", 1)])
                    qk_ev_cnt.append(P.n("aK"))
                    P.op("scalar",
                         lambda s, mb=mb, gi=gi, b=qt_sb[nqk % 4]:
                             s.dma_start(qkvT_s[mb, :, gi * 512:(gi + 1) * 512], b[:]),
                         waits=[("aK", qk_ev_cnt[nqk])], incs=[("dS", 16)])
                    qk_st_cnt.append(P.n("dS"))
                    nqk += 1
                else:
                    v_idx_of_group[j] = nv
                    w = [("tK", grp_cnt[j])]
                    if nv >= 2:
                        w.append(("tK", v_tr_cnt[nv - 2]))
                    P.op("vector", lambda v, ps=ps, b=vtmp[nv % 2]: v.tensor_copy(b[:], ps),
                         waits=w, incs=[("vK", 1)])
                    v_ev_cnt.append(P.n("vK"))
                    pend_v.append((nv, mb, gi))
                    nv += 1
            flush_vtr()

        if BUILD_PHASES >= 2:
            P.gate()
            _phases_2_to_3(nc, P, E, dict(
                qkvT_s=qkvT_s, v_s2=v_s2, wproj=wproj, x_in=x_in, xmid_s=xmid_s,
                ln2t_s=ln2t_s, ident_bf=ident_bf, ident_f32=ident_f32, ones_bf=ones_bf,
                ln_stats=ln_stats, out_d=out_d, wgu=wgu, wsgu=wsgu, gw_sb=gw_sb,
                wall_in=wall_in, wallbf_s=wallbf_s, act_s=act_s, shift_col=shift_col))

        P.gate(engines=("vector",))
        P.op("vector", lambda v: v.engine_nop())

        P.emit(blk)
    return nc


def _phases_2_to_3(nc, P, E, T):
    qkvT_s, v_s2, wproj, x_in = T["qkvT_s"], T["v_s2"], T["wproj"], T["x_in"]
    xmid_s, ln2t_s = T["xmid_s"], T["ln2t_s"]
    ident_bf, ident_f32, ones_bf = T["ident_bf"], T["ident_f32"], T["ones_bf"]
    ln_stats, out_d = T["ln_stats"], T["out_d"]
    wgu, wsgu, gw_sb = T["wgu"], T["wsgu"], T["gw_sb"]
    wall_in, wallbf_s, act_s = T["wall_in"], T["wallbf_s"], T["act_s"]
    shift_col = T["shift_col"]
    ct_stack = contextlib.ExitStack()
    cT_sb = ct_stack.enter_context(nc.sbuf_tensor("cT_sb", [8, TQ], BF16))
    ct_ev_final = [0]

    with contextlib.ExitStack() as s2x:
        E2x = s2x.enter_context
        attnout = [E2x(nc.sbuf_tensor(f"atno{h}", [128, TQ], BF16)) for h in range(NH)]
        wp_sb = [E2x(nc.sbuf_tensor(f"wp{k}", [128, 2048], BF16)) for k in range(NKC)]

        for k in range(NKC):
            P.op("gpsimd", lambda g, k=k: g.dma_start(wp_sb[k][:], wproj[k]), incs=[("dW", 16)])

        with contextlib.ExitStack() as s2:
            E2 = s2.enter_context
            khat = [E2(nc.sbuf_tensor(f"khat{i}", [128, TKV], BF16)) for i in range(2)]
            qhat = [E2(nc.sbuf_tensor(f"qhat{i}", [128, TQ], BF16)) for i in range(2)]
            v_tm = [E2(nc.sbuf_tensor(f"v_tm{i}", [128, TKV], BF16)) for i in range(2)]
            P_sb = [E2(nc.sbuf_tensor(f"P_sb{i}", [128, 1024], BF16)) for i in range(2)]
            r_sb = E2(nc.sbuf_tensor("r_sb", [128, 1024], F32))
            av_sb = E2(nc.sbuf_tensor("av_sb", [128, 1024], F32))
            den_sb = E2(nc.sbuf_tensor("den_sb", [128, 1024], F32))
            Pp_sb = [E2(nc.sbuf_tensor(f"Pp_sb{i}", [128, 1024], BF16)) for i in range(2)]
            Pq_sb = [E2(nc.sbuf_tensor(f"Pq_sb{i}", [128, 1024], BF16)) for i in range(2)]
            s_ps = [E2(nc.psum_tensor(f"s_ps{i}", [128, 1024], F32)) for i in range(2)]
            av_ps = E2(nc.psum_tensor("av_ps", [128, 1024], F32))
            den_ps = E2(nc.psum_tensor("den_ps", [128, 1024], F32))

            NKT = TKV // 128
            head_load_cnt = {}
            head_Slast = {}
            head_AVlast = {}
            s_hist, e_hist, a_hist = [], [], []
            prev_ev = 0
            prev_pairsum_last = 0
            prev_quadsum_last = 0
            prev_den_last = 0

            def load_head(h):
                par = h % 2
                w = []
                if h >= 2:
                    w = [("tK", head_Slast[h - 2]), ("tK", head_AVlast[h - 2])]
                P.op("sync", lambda sy, h=h, par=par: sy.dma_start(
                    khat[par][0:64, :], qkvT_s[8 + h // 2, (h % 2) * 64:(h % 2) * 64 + 64, :]),
                    waits=w, incs=[("dX", 16)])
                P.op("sync", lambda sy, h=h, par=par: sy.dma_start(
                    khat[par][64:128, :], qkvT_s[24 + h // 2, (h % 2) * 64:(h % 2) * 64 + 64, :]),
                    incs=[("dX", 16)])
                P.op("sync", lambda sy, h=h, par=par: sy.dma_start(
                    qhat[par][0:64, :], qkvT_s[h // 2, (h % 2) * 64:(h % 2) * 64 + 64, 0:TQ]),
                    incs=[("dX", 16)])
                P.op("sync", lambda sy, h=h, par=par: sy.dma_start(
                    qhat[par][64:128, :], qkvT_s[16 + h // 2, (h % 2) * 64:(h % 2) * 64 + 64, 0:TQ]),
                    incs=[("dX", 16)])
                P.op("sync", lambda sy, h=h, par=par: sy.dma_start(
                    v_tm[par][:].rearrange("p (tb f) -> p tb f", tb=32),
                    v_s2[h].rearrange("tb p f -> p tb f")),
                    incs=[("dX", 16)])
                head_load_cnt[h] = P.n("dX")

            load_head(0)
            load_head(1)

            for h in range(NH):
                par = h % 2
                for qt in range(2):
                    pending_avden = None
                    pairsum_cnt = {}
                    quadsum_cnt = {}
                    den_mm_cnt = {}
                    for kt in range(NKT):
                        ns = len(s_hist)
                        spar = kt % 2
                        w = [("dX", head_load_cnt[h])]
                        if ns >= 2:
                            w.append(("aK", e_hist[ns - 2]))
                        def sfn(t, spar=spar, par=par, kt=kt, qt=qt):
                            t.matmul(s_ps[spar][:, 0:512],
                                     khat[par][:, kt * 128:(kt + 1) * 128],
                                     qhat[par][:, qt * 1024:qt * 1024 + 512],
                                     start=True, stop=True)
                            return t.matmul(s_ps[spar][:, 512:1024],
                                            khat[par][:, kt * 128:(kt + 1) * 128],
                                            qhat[par][:, qt * 1024 + 512:qt * 1024 + 1024],
                                            start=True, stop=True)
                        P.op("tensor", sfn, waits=w, incs=[("tK", 1)])
                        s_hist.append(P.n("tK"))
                        if qt == 1 and kt == NKT - 1:
                            head_Slast[h] = P.n("tK")
                        w = [("tK", s_hist[ns])]
                        if ns >= 2:
                            w.append(("tK", a_hist[ns - 2]))
                            if (kt - 2) // 2 in pairsum_cnt:
                                w.append(("vK", pairsum_cnt[(kt - 2) // 2]))
                            elif prev_pairsum_last:
                                w.append(("vK", prev_pairsum_last))
                        P.op("scalar",
                             lambda s, spar=spar, kt=kt: s.activation(
                                 P_sb[kt % 2][:], s_ps[spar][:], AF.Exp, bias=shift_col[:]),
                             waits=w, incs=[("aK", 1)])
                        e_hist.append(P.n("aK"))
                        def avfn(t, par=par, kt=kt, first=(kt == 0), last=(kt == NKT - 1)):
                            t.matmul(av_ps[:, 0:512], v_tm[par][:, kt * 128:(kt + 1) * 128],
                                     P_sb[kt % 2][:, 0:512], start=first, stop=last)
                            return t.matmul(av_ps[:, 512:1024], v_tm[par][:, kt * 128:(kt + 1) * 128],
                                            P_sb[kt % 2][:, 512:1024], start=first, stop=last)
                        if pending_avden is not None:
                            pfn, pe_cnt, extra = pending_avden
                            P.op("tensor", pfn, waits=[("aK", pe_cnt)] + extra, incs=[("tK", 1)])
                            a_hist.append(P.n("tK"))
                        extra = [("vK", prev_ev)] if (kt == 0 and prev_ev) else []
                        pending_avden = (avfn, P.n("aK"), extra)
                        if kt % 2 == 1:
                            p = kt // 2
                            w = [("aK", e_hist[ns])]
                            if p >= 2 and (p - 2) // 2 in quadsum_cnt:
                                w.append(("vK", quadsum_cnt[(p - 2) // 2]))
                            elif p < 2 and prev_quadsum_last:
                                w.append(("vK", prev_quadsum_last))
                            P.op("vector", lambda v, p=p: v.tensor_tensor(
                                Pp_sb[p % 2][:], P_sb[0][:], P_sb[1][:], ALU.add),
                                waits=w, incs=[("vK", 1)])
                            pairsum_cnt[p] = P.n("vK")
                            if p % 2 == 1:
                                q = p // 2
                                w = [("vK", pairsum_cnt[p])]
                                if q >= 2 and (q - 2) in den_mm_cnt:
                                    w.append(("tK", den_mm_cnt[q - 2]))
                                elif q < 2 and prev_den_last:
                                    w.append(("tK", prev_den_last))
                                P.op("vector", lambda v, q=q: v.tensor_tensor(
                                    Pq_sb[q % 2][:], Pp_sb[0][:], Pp_sb[1][:], ALU.add),
                                    waits=w, incs=[("vK", 1)])
                                quadsum_cnt[q] = P.n("vK")
                                def denfn(t, q=q, first=(q == 0), last=(q == 7)):
                                    t.matmul(den_ps[:, 0:512], ones_bf[:], Pq_sb[q % 2][:, 0:512],
                                             start=first, stop=last)
                                    return t.matmul(den_ps[:, 512:1024], ones_bf[:],
                                                    Pq_sb[q % 2][:, 512:1024],
                                                    start=first, stop=last)
                                dw_ = [("vK", quadsum_cnt[q])]
                                if q == 0 and prev_ev:
                                    dw_.append(("vK", prev_ev))
                                P.op("tensor", denfn, waits=dw_, incs=[("tK", 1)])
                                den_mm_cnt[q] = P.n("tK")
                    pfn, pe_cnt, extra = pending_avden
                    P.op("tensor", pfn, waits=[("aK", pe_cnt)] + extra, incs=[("tK", 1)])
                    a_hist.append(P.n("tK"))
                    if qt == 1:
                        head_AVlast[h] = P.n("tK")
                    prev_pairsum_last = pairsum_cnt[NKT // 2 - 1]
                    prev_quadsum_last = quadsum_cnt[NKT // 4 - 1]
                    prev_den_last = den_mm_cnt[NKT // 4 - 1]
                    P.op("vector", lambda v: v.tensor_copy(av_sb[:], av_ps[:]),
                         waits=[("tK", a_hist[-1]), ("tK", den_mm_cnt[7])], incs=[("vG", 1)])
                    P.op("vector", lambda v: v.tensor_copy(den_sb[:], den_ps[:]),
                         waits=[("vG", P.n("vG"))], incs=[("vK", 1)])
                    prev_ev = P.n("vK")
                    P.op("vector", lambda v: v.reciprocal(r_sb[:], den_sb[:]),
                         waits=[("vK", prev_ev)], incs=[("vG", 1)])
                    P.op("vector",
                         lambda v, h=h, qt=qt: v.scalar_tensor_tensor(
                             attnout[h][:, qt * 1024:(qt + 1) * 1024], av_sb[:], 0.8, r_sb[:],
                             ALU.mult, ALU.mult),
                         waits=[("vG", P.n("vG"))], incs=[("vK", 1)])
                if h + 2 < NH:
                    load_head(h + 2)

        if BUILD_PHASES >= 3:
            P.gate()
            with contextlib.ExitStack() as s25:
                E25 = s25.enter_context
                attn_tm_t = E25(nc.sbuf_tensor("attn_tm", [128, D], F32))
                attn_tm = [attn_tm_t, attn_tm_t]
                x_tm = [E25(nc.sbuf_tensor(f"x_tm{i}", [128, D], F32)) for i in range(2)]
                xm_tm = [E25(nc.sbuf_tensor(f"xm_tm{i}", [128, D], F32)) for i in range(2)]
                tmp_tm = E25(nc.sbuf_tensor("tmp_tm", [128, D], F32))
                ln2_bf = [E25(nc.sbuf_tensor(f"ln2bf{i}", [128, D], BF16)) for i in range(2)]
                ln2t_st = [E25(nc.sbuf_tensor(f"ln2t_st{i}", [128, D], BF16)) for i in range(2)]
                sc8 = E25(nc.sbuf_tensor("sc8", [128, 8], F32))
                e8 = E25(nc.sbuf_tensor("e8", [128, 8], F32))
                scn = E25(nc.sbuf_tensor("scn", [128, 8], F32))
                ge8 = E25(nc.sbuf_tensor("ge8", [128, 8], F32))
                tmp8 = E25(nc.sbuf_tensor("tmp8", [128, 8], F32))
                comb_bf = E25(nc.sbuf_tensor("comb_bf", [128, 8], BF16))
                g_m1 = E25(nc.sbuf_tensor("g_m1", [128, 1], F32))
                g_nm = E25(nc.sbuf_tensor("g_nm", [128, 1], F32))
                g_s = E25(nc.sbuf_tensor("g_s", [128, 1], F32))
                g_r = E25(nc.sbuf_tensor("g_r", [128, 1], F32))
                g_m2 = E25(nc.sbuf_tensor("g_m2", [128, 1], F32))
                pr_ps = [E25(nc.psum_tensor(f"pr_ps{i}", [128, 1024], F32)) for i in range(2)]
                tr_ps = [E25(nc.psum_tensor(f"tr25_{i}", [128, 1024], BF16)) for i in range(2)]
                g8_ps = E25(nc.psum_tensor("g8_ps", [128, 8], F32))
                ct_ps = E25(nc.psum_tensor("ct_ps", [8, 128], BF16))

                NTS = TQ // 128
                xload, ev_cnt, mm_cnt = [], [], []
                xm_done, xm_store, l2_apply = [], [], []
                l2_tr, l2_ev, l2_st = [], [], []
                gate_mm, ct_ev = [], []

                for ti in range(NTS):
                    par = ti % 2
                    w = [("vK", xm_done[ti - 2])] if ti >= 2 else []
                    P.op("sync", lambda sy, i=ti, par=par: sy.dma_start(
                        x_tm[par][:], x_in[i * 128:(i + 1) * 128, :]), waits=w, incs=[("dX", 16)])
                    xload.append(P.n("dX"))
                    for hc in range(4):
                        ps = pr_ps[hc % 2][:, (hc // 2) * 512:(hc // 2) * 512 + 512]
                        jj = ti * 4 + hc
                        w = []
                        if jj >= 4:
                            w.append(("vK", ev_cnt[jj - 4]))
                        if hc == 0 and ti >= 1:
                            w.append(("aK", prev_stA_sq))
                        pairs = [(attnout[k][:, ti * 128:(ti + 1) * 128],
                                  wp_sb[k][:, hc * 512:(hc + 1) * 512]) for k in range(NKC)]
                        P.op("tensor", _mmgroup(ps, pairs), waits=w, incs=[("tK", 1)])
                        mm_cnt.append(P.n("tK"))
                        P.op("vector",
                             lambda v, par=par, hc=hc, ps=ps: v.tensor_copy(
                                 attn_tm[par][:, hc * 512:(hc + 1) * 512], ps),
                             waits=[("tK", mm_cnt[-1])], incs=[("vK", 1)])
                        ev_cnt.append(P.n("vK"))
                    stA = ln_stats("a", par, attn_tm[par][:], D,
                                   guard_extra=[("vK", ev_cnt[-1])])
                    prev_stA_sq = stA["aK_sq"]
                    P.op("vector",
                         lambda v, par=par, stv=stA: v.tensor_scalar(
                             tmp_tm[:], attn_tm[par][:], stv["mean"][:], stv["rstd"][:],
                             ALU.subtract, ALU.mult),
                         waits=[("aK", stA["aK_rstd"]), ("vK", stA["vK_nmr"])], incs=[("vG", 1)])
                    w = [("vG", P.n("vG")), ("dX", xload[ti])]
                    if ti >= 2:
                        w += [("dS", xm_store[ti - 2]), ("aK", l2_apply[ti - 2])]
                    P.op("vector",
                         lambda v, par=par: v.tensor_tensor(xm_tm[par][:], tmp_tm[:], x_tm[par][:], ALU.add),
                         waits=w, incs=[("vK", 1)])
                    xm_done.append(P.n("vK"))
                    P.op("scalar", lambda s, i=ti, par=par: s.dma_start(
                        xmid_s[i * 128:(i + 1) * 128, :], xm_tm[par][:]),
                        waits=[("vK", xm_done[ti])], incs=[("dS", 16)])
                    xm_store.append(P.n("dS"))
                    stB = ln_stats("b", par, xm_tm[par][:], D,
                                   guard_extra=[("vK", xm_done[ti])])
                    w = [("aK", stB["aK_rstd"]), ("vK", stB["vK_nmr"])]
                    if ti >= 2:
                        w.append(("tK", l2_tr[2 * (ti - 2) + 1]))
                    P.op("scalar",
                         lambda s, par=par, stv=stB: s.activation(
                             ln2_bf[par][:], xm_tm[par][:], AF.Identity,
                             bias=stv["nmr"][:], scale=stv["rstd"][:]),
                         waits=w, incs=[("aK", 1)])
                    l2_apply.append(P.n("aK"))
                    for rnd in range(2):
                        w = [("aK", l2_apply[ti])]
                        prev = 2 * ti + rnd - 2
                        if prev >= 0:
                            w.append(("vK", l2_ev[prev]))
                        def trfn(t, par=par, rnd=rnd):
                            last = None
                            for k in range(8):
                                kc = rnd * 8 + k
                                last = t.transpose(tr_ps[rnd][:, k * 128:(k + 1) * 128],
                                                   ln2_bf[par][:, kc * 128:(kc + 1) * 128], ident_bf[:])
                            return last
                        P.op("tensor", trfn, waits=w, incs=[("tK", 1)])
                        l2_tr.append(P.n("tK"))
                        w = [("tK", l2_tr[-1])]
                        if ti >= 2:
                            w.append(("dS", l2_st[ti - 2]))
                            w.append(("tK", gate_mm[ti - 2]))
                        P.op("vector",
                             lambda v, par=par, rnd=rnd: v.tensor_copy(
                                 ln2t_st[par][:, rnd * 1024:(rnd + 1) * 1024], tr_ps[rnd][:]),
                             waits=w, incs=[("vK", 1)])
                        l2_ev.append(P.n("vK"))
                    P.op("scalar",
                         lambda s, i=ti, par=par: s.dma_start(
                             ln2t_s[:, :, i * 128:(i + 1) * 128].rearrange("k p t -> p k t"),
                             ln2t_st[par][:].rearrange("p (k t) -> p k t", k=NKC)),
                         waits=[("vK", l2_ev[-1])], incs=[("dS", 16)])
                    l2_st.append(P.n("dS"))
                    w = [("vK", l2_ev[-1])]
                    if ti >= 1:
                        w.append(("vK", ct_ev[ti - 1]))
                    pairs = [(ln2t_st[par][:, k * 128:(k + 1) * 128],
                              gw_sb[:, k * 8:(k + 1) * 8]) for k in range(NKC)]
                    P.op("tensor", _mmgroup(g8_ps[:], pairs), waits=w, incs=[("tK", 1)])
                    gate_mm.append(P.n("tK"))

                    def vop(fn, waits=(), inc="vG"):
                        P.op("vector", fn, waits=waits, incs=[(inc, 1)])
                    P.op("scalar", lambda s: s.activation(e8[:], g8_ps[:], AF.Exp),
                         waits=[("tK", gate_mm[-1])], incs=[("aK", 1)])
                    a_e8 = P.n("aK")
                    vop(lambda v: v.tensor_reduce(g_s[:], e8[:], AX.X, ALU.add), waits=[("aK", a_e8)])
                    vop(lambda v: v.reciprocal(g_r[:], g_s[:]), waits=[("vG", P.n("vG"))])
                    vop(lambda v: v.tensor_reduce(g_m1[:], e8[:], AX.X, ALU.max))
                    vop(lambda v: v.tensor_scalar(ge8[:], e8[:], g_m1[:], None, ALU.is_ge),
                        waits=[("vG", P.n("vG"))])
                    vop(lambda v: v.scalar_tensor_tensor(tmp8[:], ge8[:], -1e30, e8[:], ALU.mult, ALU.add),
                        waits=[("vG", P.n("vG"))])
                    vop(lambda v: v.tensor_reduce(g_m2[:], tmp8[:], AX.X, ALU.max), waits=[("vG", P.n("vG"))])
                    vop(lambda v: v.tensor_scalar(ge8[:], e8[:], g_m2[:], None, ALU.is_ge),
                        waits=[("vG", P.n("vG"))])
                    vop(lambda v: v.tensor_tensor(tmp8[:], ge8[:], e8[:], ALU.mult),
                        waits=[("vG", P.n("vG"))])
                    vop(lambda v: v.tensor_scalar(comb_bf[:], tmp8[:], g_r[:], None, ALU.mult),
                        waits=[("vG", P.n("vG"))], inc="vK")
                    v_comb = P.n("vK")
                    w = [("vK", v_comb)]
                    if ti >= 1:
                        w.append(("vK", ct_ev[ti - 1]))
                    P.op("tensor", lambda t: t.transpose(ct_ps[:], comb_bf[:], ident_bf[:]),
                         waits=w, incs=[("tK", 1)])
                    t_ct = P.n("tK")
                    P.op("vector", lambda v, i=ti: v.tensor_copy(cT_sb[0:8, i * 128:(i + 1) * 128], ct_ps[:]),
                         waits=[("tK", t_ct)], incs=[("vK", 1)])
                    ct_ev.append(P.n("vK"))
                ct_ev_final[0] = ct_ev[-1]

    if BUILD_PHASES < 4:
        return
    P.gate()
    with contextlib.ExitStack() as s3a:
        E3 = s3a.enter_context
        ln2T = E3(nc.sbuf_tensor("ln2T", [128, NKC * TQ], BF16))
        act_sb_t = E3(nc.sbuf_tensor("act_sb", [128, NFC_E * TQ], BF16))
        act_sb = [act_sb_t, act_sb_t]
        wgsl = [E3(nc.sbuf_tensor(f"wgsl{i}", [128, 2048], BF16)) for i in range(4)]
        c_bc = E3(nc.sbuf_tensor("c_bc", [128, TQ], F32))
        silu_sb = [E3(nc.sbuf_tensor(f"silu{i}", [128, 512], F32)) for i in range(2)]
        t1_sb = [E3(nc.sbuf_tensor(f"t1_{i}", [128, 512], F32)) for i in range(2)]
        crow = E3(nc.sbuf_tensor("crow", [1, TQ], BF16))
        wcast = [E3(nc.sbuf_tensor(f"wcast{i}", [128, 22 * 128], BF16)) for i in range(2)]
        cb_ps = E3(nc.psum_tensor("cb_ps", [128, 512], F32))
        guA = E3(nc.psum_tensor("guA", [128, 1024], F32))
        guB = E3(nc.psum_tensor("guB", [128, 1024], F32))

        NTS = TQ // 128

        P.op("sync", lambda sy: sy.dma_start(
            ln2T[:].rearrange("p (k t) -> p k t", k=NKC),
            ln2t_s.rearrange("k p t -> p k t")), incs=[("dX", 16)])
        d_ln2T = P.n("dX")

        def slab_ap(e, idx):
            if e < NE:
                return wgu[e, idx, :, :]
            half = e - NE
            if idx < 11:
                return wsgu[half * 11 + idx, :, :]
            return wsgu[22 + half * 11 + (idx - 11), :, :]

        def slab_block(e, idx):
            m, which = divmod(idx, 2)
            return slab_ap(e, m if which == 0 else 11 + m)

        pc_loads, pc_stores = [], []
        pc_state = {"n": 0, "pending": None}

        def emit_precast_chunk():
            n = pc_state["n"]
            if pc_state["pending"] is not None:
                hc, q, buf, dcnt = pc_state["pending"]
                P.op("scalar", lambda s, hc=hc, q=q, b=buf: s.dma_start(
                    wallbf_s[hc, q * 22:(q + 1) * 22].rearrange("fc p f -> p fc f"),
                    b[:].rearrange("p (fc f) -> p fc f", fc=22)),
                    waits=[("dW", dcnt)], incs=[("dS", 16)])
                pc_stores.append(P.n("dS"))
                pc_state["pending"] = None
            if n >= 80:
                return
            hc, q = divmod(n, 5)
            w = []
            if n >= 2:
                w.append(("dS", pc_stores[n - 2]))
            buf = wcast[n % 2]
            P.op("gpsimd", lambda g, hc=hc, q=q, b=buf: g.dma_start(
                b[:].rearrange("p (fc f) -> p fc f", fc=22),
                wall_in[hc, q * 22:(q + 1) * 22].rearrange("fc p f -> p fc f")),
                waits=w, incs=[("dW", 16)])
            pc_loads.append(P.n("dW"))
            pc_state["pending"] = (hc, q, buf, P.n("dW"))
            pc_state["n"] = n + 1

        slab_cnt = {}
        slab_used = {}
        act_store = []
        act_last_ev = {}
        jn_hist = []
        cb_prev = 0

        def load_slab(e, idx):
            w = []
            if idx >= 4:
                kp = (e, idx - 4)
            elif e >= 1:
                kp = (e - 1, 18 + ((idx + 2) % 4))
            else:
                kp = None
            if kp is not None and kp in slab_used:
                w.append(("tK", slab_used[kp]))
            P.op("gpsimd", lambda g, ap=slab_block(e, idx), b=wgsl[idx % 4]: g.dma_start(b[:], ap),
                 waits=w, incs=[("dW", 16)])
            slab_cnt[(e, idx)] = P.n("dW")

        for idx in range(4):
            load_slab(0, idx)

        cb_mm_last = 0
        for e in range(N_ELOOP):
            if e < NE:
                w = [("vK", ct_ev_final[0])]
                if cb_mm_last:
                    w.append(("tK", cb_mm_last))
                P.op("sync", lambda sy, e=e: sy.dma_start(crow[:], cT_sb[e:e + 1, :]),
                     waits=w, incs=[("dX", 16)])
                d_crow = P.n("dX")
                cb_done = {}
                for nt in range(4):
                    w = [("dX", d_crow)]
                    if nt >= 1:
                        w.append(("aK", cb_done[nt - 1]))
                    P.op("tensor", lambda t, e=e, nt=nt: t.matmul(
                        cb_ps[:], ones_bf[0:1, :],
                        crow[0:1, nt * 512:(nt + 1) * 512],
                        start=True, stop=True), waits=w, incs=[("tK", 1)])
                    cb_mm_last = P.n("tK")
                    tcb = P.n("tK")
                    w = [("tK", tcb)]
                    if nt == 0 and cb_prev:
                        w.append(("vK", cb_prev))
                    P.op("scalar", lambda s, nt=nt: s.activation(
                        c_bc[:, nt * 512:(nt + 1) * 512], cb_ps[:], AF.Copy),
                        waits=w, incs=[("aK", 1)])
                    cb_done[nt] = P.n("aK")
                cb_ready = cb_done[3]
            else:
                cb_ready = 0

            for m in range(NFC_E):
                i1, i2 = 2 * m, 2 * m + 1
                for nt in range(4):
                    ps1 = guA[:, (nt % 2) * 512:(nt % 2) * 512 + 512]
                    ps2 = guB[:, (nt % 2) * 512:(nt % 2) * 512 + 512]
                    jn = len(jn_hist)
                    w = [("dW", slab_cnt[(e, i1)]), ("dW", slab_cnt[(e, i2)]),
                         ("dX", d_ln2T)]
                    if jn >= 2:
                        w.append(("vK", jn_hist[jn - 2]))
                    def gufn(t, e=e, m=m, nt=nt, ps1=ps1, ps2=ps2):
                        last = None
                        for k in range(NKC):
                            t.matmul(ps1, wgsl[(2 * m) % 4][:, k * 128:(k + 1) * 128],
                                     ln2T[:, k * TQ + nt * 512: k * TQ + (nt + 1) * 512],
                                     start=(k == 0), stop=(k == NKC - 1))
                        for k in range(NKC):
                            last = t.matmul(ps2, wgsl[(2 * m + 1) % 4][:, k * 128:(k + 1) * 128],
                                            ln2T[:, k * TQ + nt * 512: k * TQ + (nt + 1) * 512],
                                            start=(k == 0), stop=(k == NKC - 1))
                        return last
                    P.op("tensor", gufn, waits=w, incs=[("tK", 1)])
                    slab_used[(e, i1)] = slab_used[(e, i2)] = P.n("tK")
                    tmm = P.n("tK")
                    P.op("scalar", lambda s, ps1=ps1, b=silu_sb[nt % 2]: s.activation(b[:], ps1, AF.Silu),
                         waits=[("tK", tmm)], incs=[("aK", 1)])
                    a_silu = P.n("aK")
                    P.op("vector", lambda v, ps2=ps2, b=silu_sb[nt % 2], o=t1_sb[nt % 2]:
                         v.tensor_tensor(o[:], b[:], ps2, ALU.mult),
                         waits=[("aK", a_silu)], incs=[("vG", 1)])
                    dst = act_sb[e % 2][:, m * TQ + nt * 512: m * TQ + (nt + 1) * 512]
                    w = [("vG", P.n("vG"))]
                    if e >= 1 and m == 0 and nt == 0:
                        w.append(("dS", act_store[e - 1]))
                    if e < NE:
                        w.append(("aK", cb_ready))
                        P.op("vector", lambda v, dst=dst, i=t1_sb[nt % 2], nt=nt:
                             v.tensor_tensor(dst, i[:], c_bc[:, nt * 512:(nt + 1) * 512], ALU.mult),
                             waits=w, incs=[("vK", 1)])
                    else:
                        P.op("vector", lambda v, dst=dst, i=t1_sb[nt % 2]: v.tensor_copy(dst, i[:]),
                             waits=w, incs=[("vK", 1)])
                    jn_hist.append(P.n("vK"))
                for nxt in (2 * m + 4, 2 * m + 5):
                    if nxt < 22:
                        load_slab(e, nxt)
                emit_precast_chunk()
            if e + 1 < N_ELOOP:
                for idx in range(4):
                    load_slab(e + 1, idx)
            act_last_ev[e] = P.n("vK")
            cb_prev = P.n("vK")
            rows0 = e * NFC_E
            for tts in range(4):
                P.op("scalar", lambda s, rows0=rows0, b=act_sb[e % 2], tts=tts: s.dma_start(
                    act_s[tts, :, rows0:rows0 + NFC_E, :],
                    b[:].rearrange("p (fc tok) -> p fc tok", fc=NFC_E)[:, :, tts * 512:(tts + 1) * 512]),
                    waits=[("vK", act_last_ev[e])], incs=[("dS", 16)])
            act_store.append(P.n("dS"))
        while pc_state["n"] < 80 or pc_state["pending"] is not None:
            emit_precast_chunk()

    if BUILD_PHASES < 5:
        ct_stack.close()
        return
    ct_stack.close()
    P.gate()
    with contextlib.ExitStack() as s3b:
        E3b = s3b.enter_context
        act_tt = E3b(nc.sbuf_tensor("act_tt", [128, NFC_ALL * 512], BF16))
        wl = [E3b(nc.sbuf_tensor(f"wl{i}", [128, NFC_ALL * 128], BF16)) for i in range(2)]
        ytmp = [E3b(nc.sbuf_tensor(f"ytmp{i}", [128, 512], F32)) for i in range(2)]
        y_tm = E3b(nc.sbuf_tensor("y_tm", [128, 4 * D], BF16))
        xm512 = [E3b(nc.sbuf_tensor(f"xm512_{i}", [128, 512], F32)) for i in range(2)]
        o512 = [E3b(nc.sbuf_tensor(f"o512_{i}", [128, 512], F32)) for i in range(2)]
        tmp512 = E3b(nc.sbuf_tensor("tmp512", [128, 512], F32))
        dn_ps = E3b(nc.psum_tensor("dn_ps", [128, 1024], F32))
        yt_ps = E3b(nc.psum_tensor("yt_ps", [128, 1024], F32))

        act_cnt = []
        wl_cnt, dn_cnt, yev_cnt, ytr_cnt, yte_cnt = {}, {}, {}, {}, {}
        out_cnt = []
        tt_last_mm = []
        add_hist = []
        last_sq = 0
        pend_y = []

        def flush_ytr():
            while pend_y:
                key = pend_y.pop(0)
                ktt, khc = key
                kpk = (ktt, khc - 2) if khc >= 2 else ((ktt - 1, khc + 14) if ktt >= 1 else None)
                w = [("vK", yev_cnt[key]), ("gK", 1)]
                if kpk in yte_cnt:
                    w.append(("vK", yte_cnt[kpk]))
                def ytr(t, hc=khc):
                    last = None
                    for tb in range(4):
                        last = t.transpose(
                            yt_ps[:, (hc % 2) * 512 + tb * 128:(hc % 2) * 512 + (tb + 1) * 128],
                            ytmp[hc % 2][:, tb * 128:(tb + 1) * 128], ident_f32[:])
                    return last
                P.op("tensor", ytr, waits=w, incs=[("tK", 1)])
                ytr_cnt[key] = P.n("tK")
                P.op("vector", lambda v, hc=khc: v.tensor_copy(
                    y_tm[:].rearrange("p (tb f) -> p tb f", tb=4)[:, :, hc * 128:(hc + 1) * 128],
                    yt_ps[:, (hc % 2) * 512:(hc % 2) * 512 + 512].rearrange("p (tb f) -> p tb f", tb=4)),
                    waits=[("tK", ytr_cnt[key])], incs=[("vK", 1)])
                yte_cnt[key] = P.n("vK")

        for tt in range(4):
            w = []
            if tt >= 1:
                w.append(("tK", tt_last_mm[tt - 1]))
            P.op("sync", lambda sy, tt=tt: sy.dma_start(
                act_tt[:].rearrange("p (fc t) -> p fc t", fc=NFC_ALL),
                act_s[tt]),
                waits=w, incs=[("dX", 16)])
            act_cnt.append(P.n("dX"))
            for hc in range(16):
                key = (tt, hc)
                pk = (tt, hc - 2) if hc >= 2 else ((tt - 1, hc + 14) if tt >= 1 else None)
                w = []
                if pk in dn_cnt:
                    w.append(("tK", dn_cnt[pk]))
                P.op("sync", lambda g, hc=hc, b=wl[hc % 2]: g.dma_start(
                    b[:].rearrange("p (fc f) -> p fc f", fc=NFC_ALL),
                    wallbf_s[hc].rearrange("fc p f -> p fc f")),
                    waits=w, incs=[("dX", 16)])
                wl_cnt[key] = P.n("dX")
                ps = dn_ps[:, (hc % 2) * 512:(hc % 2) * 512 + 512]
                w = [("dX", wl_cnt[key]), ("dX", act_cnt[tt])]
                if pk in yev_cnt:
                    w.append(("vK", yev_cnt[pk]))
                pairs = [(wl[hc % 2][:, fc * 128:(fc + 1) * 128],
                          act_tt[:, fc * 512:(fc + 1) * 512]) for fc in range(NFC_ALL)]
                P.op("tensor", _mmgroup(ps, pairs), waits=w, incs=[("tK", 1)])
                dn_cnt[key] = P.n("tK")
                if hc == 15:
                    tt_last_mm.append(P.n("tK"))
                flush_ytr()
                w = [("tK", dn_cnt[key])]
                if pk in ytr_cnt:
                    w.append(("tK", ytr_cnt[pk]))
                if hc == 0 and last_sq:
                    w.append(("aK", last_sq))
                P.op("vector", lambda v, ps=ps, b=ytmp[hc % 2]: v.tensor_copy(b[:], ps),
                     waits=w, incs=[("vK", 1)])
                yev_cnt[key] = P.n("vK")
                pend_y.append(key)
            flush_ytr()
            for tb in range(4):
                ysl = y_tm[:, tb * D:(tb + 1) * D]
                stF = ln_stats("a", tb % 2, ysl, D, guard_extra=[("vK", yte_cnt[(tt, 15)])])
                last_sq = stF["aK_sq"]
                for cc in range(4):
                    rows0 = tt * 512 + tb * 128
                    na = len(add_hist)
                    w = [("vK", add_hist[na - 2])] if na >= 2 else []
                    P.op("sync", lambda sy, rows0=rows0, cc=cc, b=xm512[cc % 2]: sy.dma_start(
                        b[:], xmid_s[rows0:rows0 + 128, cc * 512:(cc + 1) * 512]),
                        waits=w, incs=[("dX", 16)])
                    d_xm = P.n("dX")
                    P.op("vector", lambda v, ysl=ysl, cc=cc, stv=stF: v.tensor_scalar(
                        tmp512[:], ysl[:, cc * 512:(cc + 1) * 512], stv["mean"][:], stv["rstd"][:],
                        ALU.subtract, ALU.mult),
                        waits=[("aK", stF["aK_rstd"]), ("vK", stF["vK_nmr"])], incs=[("vG", 1)])
                    no = len(out_cnt)
                    w = [("vG", P.n("vG")), ("dX", d_xm)]
                    if no >= 2:
                        w.append(("dO", out_cnt[no - 2]))
                    P.op("vector", lambda v, cc=cc, b=xm512[cc % 2], o=o512[cc % 2]:
                         v.tensor_tensor(o[:], tmp512[:], b[:], ALU.add),
                         waits=w, incs=[("vK", 1)])
                    add_hist.append(P.n("vK"))
                    P.op("scalar", lambda s, rows0=rows0, cc=cc, o=o512[cc % 2]: s.dma_start(
                        out_d[rows0:rows0 + 128, cc * 512:(cc + 1) * 512], o[:]),
                        waits=[("vK", add_hist[-1])], incs=[("dO", 16)])
                    out_cnt.append(P.n("dO"))



def _prep_shared(inputs):
    W_attn = np.asarray(inputs["L0_attn_W_attn"], np.float32)
    W_proj = np.asarray(inputs["L0_attn_W_proj"], np.float32)
    gu = np.asarray(inputs["L0_ffn_gate_up_weights"], np.float32)
    dw = np.asarray(inputs["L0_ffn_down_weights"], np.float32)
    sgu = np.asarray(inputs["L0_ffn_shared_gate_up"], np.float32)
    sd = np.asarray(inputs["L0_ffn_shared_down"], np.float32)
    gw = np.asarray(inputs["L0_ffn_gate_weight"], np.float32)

    qk = np.ascontiguousarray(
        W_attn[:, :4096].reshape(16, 128, 32, 128).transpose(2, 1, 0, 3).reshape(32, 128, 2048))
    vv = np.ascontiguousarray(
        W_attn[:, 4096:].reshape(16, 128, 16, 128).transpose(2, 1, 0, 3).reshape(16, 128, 2048))
    wattn = np.concatenate([qk, vv], axis=0)

    wproj = np.ascontiguousarray(W_proj.reshape(16, 128, 2048))

    lam = np.stack([np.asarray(inputs[f"L0_attn_lambda_{n}"], np.float32)
                    for n in ("q1", "k1", "q2", "k2")])
    lam_in = np.ascontiguousarray(lam.reshape(4, 16, 128).transpose(2, 0, 1).reshape(128, 64))

    gw_in = np.ascontiguousarray(gw.T.reshape(16, 128, 8))

    wgu_h = np.ascontiguousarray(
        gu.transpose(0, 2, 1).reshape(NE, 16, 128, 22, 128).transpose(0, 3, 2, 1, 4)
        .reshape(NE, 22, 128, 2048))
    wsgu_h = np.ascontiguousarray(
        sgu.T.reshape(16, 128, 44, 128).transpose(2, 1, 0, 3).reshape(44, 128, 2048))

    wall_t = np.concatenate([dw.transpose(0, 2, 1).reshape(NE * 1408, 2048), sd.T], axis=0)
    wall_h = np.ascontiguousarray(wall_t.reshape(NFC_ALL, 128, 16, 128).transpose(2, 0, 1, 3))

    return dict(wattn=wattn, wproj=wproj, lam_in=lam_in, gw_in=gw_in,
                wgu=wgu_h, wsgu=wsgu_h, wall_in=wall_h)


def make_in_maps(inputs):
    shared = _prep_shared(inputs)
    x = np.asarray(inputs["L0_attn_x"], np.float32)
    in_maps = []
    for c in range(8):
        b, r = divmod(c, 2)
        seq = x[b]
        xc = np.ascontiguousarray(
            np.concatenate([seq[r * TQ:(r + 1) * TQ], seq[(1 - r) * TQ:(2 - r) * TQ]], axis=0))
        m = {"x_in": xc}
        m.update(shared)
        in_maps.append(m)
    return in_maps


def kernel(**inputs):
    in_maps = make_in_maps(inputs)
    nc = build_nc()
    res = run_bass_kernel_spmd(nc, in_maps, core_ids=list(range(8)))
    x = np.asarray(inputs["L0_attn_x"], np.float32)
    out = np.empty_like(x)
    for c in range(8):
        b, r = divmod(c, 2)
        out[b, r * TQ:(r + 1) * TQ, :] = np.asarray(res.results[c]["out"])
    return out


# version 08
# speedup vs baseline: 1.0059x; 1.0059x over previous
import contextlib
import numpy as np
import concourse.bass as bass
import concourse.mybir as mybir
from concourse.bass_utils import run_bass_kernel_spmd

F32 = mybir.dt.float32
BF16 = mybir.dt.bfloat16
AF = mybir.ActivationFunctionType
ALU = mybir.AluOpType
AX = mybir.AxisListType

D = 2048
TKV = 4096
TQ = 2048
NH = 16
NKC = 16
EPS = 1e-5
LAM_INIT = 0.2
EXP_SHIFT = -12.0
NE = 8
NFC_E = 11
NFC_ALL = 110
N_ELOOP = 10

BUILD_PHASES = 5
DEBUG_SCRATCH = False

ENGINES = ("sync", "gpsimd", "scalar", "vector", "tensor")


class Prog:

    def __init__(self, nc, stack):
        self.nc = nc
        self.ops = []
        self.cnt = {}
        self.sems = {}
        self.pending_gate = {e: [] for e in ENGINES}
        self._stack = stack

    def sem(self, name):
        if name not in self.sems:
            self.sems[name] = self._stack.enter_context(self.nc.semaphore(name))
            self.cnt[name] = 0
        return name

    def n(self, name):
        return self.cnt.get(name, 0)

    def op(self, engine, fn, waits=(), incs=()):
        w = list(self.pending_gate[engine]) + [(s, v) for (s, v) in waits if v > 0]
        self.pending_gate[engine] = []
        inc_list = []
        for (s, k) in incs:
            self.sem(s)
            self.cnt[s] += k
            inc_list.append((s, k))
        for (s, _) in w:
            self.sem(s)
        self.ops.append((engine, fn, w, inc_list))

    def gate(self, engines=ENGINES, sems=None):
        sems = sems if sems is not None else list(self.cnt.keys())
        for e in engines:
            for s in sems:
                v = self.cnt.get(s, 0)
                if v > 0:
                    self.pending_gate[e].append((s, v))

    def emit(self, blk):
        prog = self
        for ename in ENGINES:
            def make(ename):
                def section(eng):
                    waited = {}
                    for (en, fn, waits, incs) in prog.ops:
                        if en != ename:
                            continue
                        for (s, v) in waits:
                            if v > waited.get(s, 0):
                                eng.wait_ge(prog.sems[s], v)
                                waited[s] = v
                        inst = fn(eng)
                        for (s, k) in incs:
                            inst.then_inc(prog.sems[s], k)
                return section
            getattr(blk, ename)(make(ename))


def _mmgroup(ps, pairs):
    def fn(t):
        last = None
        for i, (lhs, rhs) in enumerate(pairs):
            last = t.matmul(ps, lhs, rhs, start=(i == 0), stop=(i == len(pairs) - 1))
        return last
    return fn


def build_nc():
    nc = bass.Bass("TRN2", target_bir_lowering=False, debug=False)
    io_kind = "ExternalOutput" if DEBUG_SCRATCH else "Internal"

    x_in = nc.dram_tensor("x_in", [TKV, D], F32, kind="ExternalInput").ap()
    wattn = nc.dram_tensor("wattn", [48, 128, 2048], F32, kind="ExternalInput").ap()
    wproj = nc.dram_tensor("wproj", [16, 128, 2048], F32, kind="ExternalInput").ap()
    lam_in = nc.dram_tensor("lam_in", [128, 64], F32, kind="ExternalInput").ap()
    gw_in = nc.dram_tensor("gw_in", [16, 128, 8], F32, kind="ExternalInput").ap()
    wgu = nc.dram_tensor("wgu", [NE, 22, 128, 2048], F32, kind="ExternalInput").ap()
    wsgu = nc.dram_tensor("wsgu", [44, 128, 2048], F32, kind="ExternalInput").ap()
    wall_in = nc.dram_tensor("wall_in", [16, NFC_ALL, 128, 128], F32, kind="ExternalInput").ap()

    out_d = nc.dram_tensor("out", [TQ, D], F32, kind="ExternalOutput").ap()

    qkvT_s = nc.dram_tensor("qkvT_s", [48, 128, TKV], BF16, kind=io_kind).ap()
    v_s2 = nc.dram_tensor("v_s2", [16, 32, 128, 128], BF16, kind=io_kind).ap()
    xmid_s = nc.dram_tensor("xmid_s", [TQ, D], F32, kind=io_kind).ap()
    ln2t_s = nc.dram_tensor("ln2t_s", [16, 128, TQ], BF16, kind=io_kind).ap()
    act_s = nc.dram_tensor("act_s", [4, 128, NFC_ALL, 512], BF16, kind=io_kind).ap()
    wallbf_s = nc.dram_tensor("wallbf_s", [16, NFC_ALL, 128, 128], BF16, kind="Internal").ap()

    st = contextlib.ExitStack()
    E = st.enter_context
    with st:
        ident_bf = E(nc.sbuf_tensor("ident_bf", [128, 128], BF16))
        ident_f32 = E(nc.sbuf_tensor("ident_f32", [128, 128], F32))
        ones_bf = E(nc.sbuf_tensor("ones_bf", [128, 128], BF16))
        lam_sb = E(nc.sbuf_tensor("lam_sb", [128, 64], F32))
        prod2 = E(nc.sbuf_tensor("prod2", [128, 32], F32))
        rowsum2f = E(nc.sbuf_tensor("rowsum2f", [128, 2], F32))
        rowsum2 = E(nc.sbuf_tensor("rowsum2", [128, 2], BF16))
        expd = E(nc.sbuf_tensor("expd", [1, 2], F32))
        lam_val = E(nc.sbuf_tensor("lam_val", [1, 1], F32))
        lam_val_bf = E(nc.sbuf_tensor("lam_val_bf", [1, 1], BF16))
        lamq2_col = E(nc.sbuf_tensor("lamq2_col", [128, 1], F32))
        gw_sb = E(nc.sbuf_tensor("gw_sb", [128, 128], BF16))
        sq_scrap = E(nc.sbuf_tensor("sq_scrap", [128, 2048], F32))
        eps_col = E(nc.sbuf_tensor("eps_col", [128, 1], F32))
        shift_col = E(nc.sbuf_tensor("shift_col", [128, 1], F32))
        COLS = {}
        for nm in ("sum", "ssq", "mean", "m2", "var", "rstd", "nmr"):
            for grp in ("a", "b"):
                COLS[(grp, nm)] = [E(nc.sbuf_tensor(f"c_{grp}{nm}{i}", [128, 1], F32))
                                   for i in range(2)]

        P = Prog(nc, st)
        for s in ("dX", "dW", "dS", "dO", "dC", "tK", "vK", "aK", "gK", "vG", "aG"):
            P.sem(s)

        blk = E(nc.Block())

        ln_last_var = {}

        def ln_stats(grp, par, x_ap, n_free, guard_extra=()):
            c = lambda nm: COLS[(grp, nm)][par]
            war = [("vK", ln_last_var.get((grp, par), 0))]
            P.op("scalar",
                 lambda s, x=x_ap, o=c("ssq"): s.activation(
                     sq_scrap[:, 0:n_free], x, AF.Square, accum_out=o[:]),
                 waits=list(guard_extra) + war, incs=[("aK", 1)])
            a_ssq = P.n("aK")
            P.op("vector",
                 lambda v, x=x_ap, o=c("sum"): v.tensor_reduce(o[:], x, AX.X, ALU.add),
                 waits=guard_extra, incs=[("vG", 1)])
            P.op("vector",
                 lambda v, i=c("sum"), o=c("mean"): v.tensor_scalar_mul(o[:], i[:], 1.0 / n_free),
                 waits=[("vG", P.n("vG"))], incs=[("vG", 1)])
            P.op("vector",
                 lambda v, i=c("mean"), o=c("m2"): v.tensor_tensor(o[:], i[:], i[:], ALU.mult),
                 waits=[("vG", P.n("vG"))], incs=[("vG", 1)])
            P.op("vector",
                 lambda v, i=c("ssq"), m=c("m2"), o=c("var"):
                     v.scalar_tensor_tensor(o[:], i[:], 1.0 / n_free, m[:], ALU.mult, ALU.subtract),
                 waits=[("vG", P.n("vG")), ("aK", a_ssq)], incs=[("vK", 1)])
            v_var = P.n("vK")
            ln_last_var[(grp, par)] = v_var
            P.op("scalar",
                 lambda s, i=c("var"), o=c("rstd"): s.activation(o[:], i[:], AF.Ln, bias=eps_col[:]),
                 waits=[("vK", v_var)], incs=[("aG", 1)])
            P.op("scalar",
                 lambda s, io=c("rstd"): s.activation(io[:], io[:], AF.Exp, scale=-0.5),
                 waits=[("aG", P.n("aG"))], incs=[("aK", 1)])
            a_rstd = P.n("aK")
            P.op("vector",
                 lambda v, m=c("mean"), r=c("rstd"), o=c("nmr"):
                     v.scalar_tensor_tensor(o[:], m[:], -1.0, r[:], ALU.mult, ALU.mult),
                 waits=[("aK", a_rstd)], incs=[("vK", 1)])
            return {"aK_sq": a_ssq, "aK_rstd": a_rstd, "vK_nmr": P.n("vK"),
                    "mean": c("mean"), "rstd": c("rstd"), "nmr": c("nmr")}

        P.op("gpsimd", lambda g: g.memset(ident_bf[:], 0.0), incs=[("gK", 1)])
        P.op("gpsimd", lambda g: g.affine_select(out=ident_bf[:], in_=ident_bf[:],
                                                 compare_op=ALU.not_equal, fill=1.0, base=0,
                                                 pattern=[[-1, 128]], channel_multiplier=1),
             waits=[("gK", P.n("gK"))])
        P.op("gpsimd", lambda g: g.memset(ident_f32[:], 0.0), incs=[("gK", 1)])
        P.op("gpsimd", lambda g: g.affine_select(out=ident_f32[:], in_=ident_f32[:],
                                                 compare_op=ALU.not_equal, fill=1.0, base=0,
                                                 pattern=[[-1, 128]], channel_multiplier=1),
             waits=[("gK", P.n("gK"))])
        P.op("gpsimd", lambda g: g.memset(eps_col[:], EPS))
        P.op("gpsimd", lambda g: g.memset(shift_col[:], EXP_SHIFT))
        P.op("gpsimd", lambda g: g.memset(ones_bf[:], 1.0), incs=[("gK", 1)])
        g_const = P.n("gK")
        P.op("gpsimd", lambda g: g.dma_start(lam_sb[:], lam_in), incs=[("dW", 16)])
        P.op("gpsimd", lambda g: g.dma_start(gw_sb[:].rearrange("p (a e) -> p a e", a=16),
                                             gw_in.rearrange("a p e -> p a e")), incs=[("dW", 16)])
        d_lamgw = P.n("dW")

        ph0_ps = contextlib.ExitStack()
        lam_ps = ph0_ps.enter_context(nc.psum_tensor("lam_ps", [1, 2], F32))
        lam_bc_ps = ph0_ps.enter_context(nc.psum_tensor("lam_bc_ps", [128, 1], F32))

        P.op("vector", lambda v: v.tensor_tensor(prod2[:, 0:16], lam_sb[:, 0:16], lam_sb[:, 16:32], ALU.mult),
             waits=[("dW", d_lamgw)])
        P.op("vector", lambda v: v.tensor_tensor(prod2[:, 16:32], lam_sb[:, 32:48], lam_sb[:, 48:64], ALU.mult),
             incs=[("vG", 1)])
        P.op("vector", lambda v: v.tensor_reduce(rowsum2f[:, 0:1], prod2[:, 0:16], AX.X, ALU.add),
             waits=[("vG", P.n("vG"))])
        P.op("vector", lambda v: v.tensor_reduce(rowsum2f[:, 1:2], prod2[:, 16:32], AX.X, ALU.add),
             incs=[("vG", 1)])
        P.op("vector", lambda v: v.tensor_copy(rowsum2[:], rowsum2f[:]),
             waits=[("vG", P.n("vG"))], incs=[("vK", 1)])
        P.op("tensor", lambda t: t.matmul(lam_ps[:], ones_bf[:, 0:1], rowsum2[:], start=True, stop=True),
             waits=[("gK", g_const), ("vK", P.n("vK"))], incs=[("tK", 1)])
        P.op("scalar", lambda s: s.activation(expd[:], lam_ps[:], AF.Exp),
             waits=[("tK", P.n("tK"))], incs=[("aK", 1)])
        P.op("vector", lambda v: v.tensor_tensor(lam_val[:], expd[:, 0:1], expd[:, 1:2], ALU.subtract),
             waits=[("aK", P.n("aK"))], incs=[("vG", 1)])
        P.op("vector", lambda v: v.tensor_scalar_add(lam_val[:], lam_val[:], LAM_INIT),
             waits=[("vG", P.n("vG"))], incs=[("vG", 1)])
        P.op("vector", lambda v: v.tensor_copy(lam_val_bf[:], lam_val[:]),
             waits=[("vG", P.n("vG"))], incs=[("vK", 1)])
        P.op("tensor", lambda t: t.matmul(lam_bc_ps[:], ones_bf[0:1, :], lam_val_bf[:], start=True, stop=True),
             waits=[("vK", P.n("vK"))], incs=[("tK", 1)])
        P.op("vector", lambda v: v.tensor_scalar_mul(lamq2_col[:], lam_bc_ps[:], -0.125),
             waits=[("tK", P.n("tK"))], incs=[("vK", 1)])
        v_lamq2 = P.n("vK")
        ph0_ps.close()

        with contextlib.ExitStack() as ph1:
            E1 = ph1.enter_context
            lnT_full = E1(nc.sbuf_tensor("lnT_full", [128, NKC * TKV], BF16))
            xa = [E1(nc.sbuf_tensor(f"xa{i}", [128, D], F32)) for i in range(2)]
            ln_tm = [E1(nc.sbuf_tensor(f"ln_tm{i}", [128, D], BF16)) for i in range(2)]
            wslab = [E1(nc.sbuf_tensor(f"wslab{i}", [128, 2048], BF16)) for i in range(2)]
            qt_sb = [E1(nc.sbuf_tensor(f"qt_sb{i}", [128, 512], BF16)) for i in range(4)]
            vtmp = [E1(nc.sbuf_tensor(f"vtmp{i}", [128, 512], BF16)) for i in range(2)]
            vt_sb = [E1(nc.sbuf_tensor(f"vt_sb{i}", [128, 512], BF16)) for i in range(2)]
            ps_qkv = [E1(nc.psum_tensor(f"ps_qkv{i}", [128, 1024], F32)) for i in range(2)]
            ps_tr = [E1(nc.psum_tensor(f"ps_tr{i}", [128, 1024], BF16)) for i in range(2)]

            NT_A = TKV // 128
            lnT_r = lnT_full[:].rearrange("p (k t) -> p k t", k=NKC)

            apply_cnt, tr_cnt, ev_cnt, load_cnt = [], [], [], []
            for i in range(NT_A):
                par = i % 2
                w = [("aK", apply_cnt[i - 2])] if i >= 2 else []
                P.op("sync", lambda sy, i=i, par=par: sy.dma_start(
                    xa[par][:], x_in[i * 128:(i + 1) * 128, :]), waits=w, incs=[("dX", 16)])
                load_cnt.append(P.n("dX"))
                st_i = ln_stats("a", par, xa[par][:], D, guard_extra=[("dX", load_cnt[i])])
                w = [("vK", st_i["vK_nmr"])]
                if i >= 2:
                    w.append(("tK", tr_cnt[2 * (i - 2) + 1]))
                P.op("scalar",
                     lambda s, par=par, stv=st_i: s.activation(
                         ln_tm[par][:], xa[par][:], AF.Identity, bias=stv["nmr"][:], scale=stv["rstd"][:]),
                     waits=w, incs=[("aK", 1)])
                apply_cnt.append(P.n("aK"))
                for rnd in range(2):
                    w = [("aK", apply_cnt[i]), ("gK", g_const)]
                    if i == 0:
                        w.append(("vK", v_lamq2))
                    prev = 2 * i + rnd - 2
                    if prev >= 0:
                        w.append(("vK", ev_cnt[prev]))
                    def trfn(t, par=par, rnd=rnd):
                        last = None
                        for k in range(8):
                            kc = rnd * 8 + k
                            last = t.transpose(ps_tr[rnd][:, k * 128:(k + 1) * 128],
                                               ln_tm[par][:, kc * 128:(kc + 1) * 128], ident_bf[:])
                        return last
                    P.op("tensor", trfn, waits=w, incs=[("tK", 1)])
                    tr_cnt.append(P.n("tK"))
                    P.op("vector",
                         lambda v, i=i, rnd=rnd: v.tensor_copy(
                             lnT_r[:, rnd * 8:(rnd + 1) * 8, i * 128:(i + 1) * 128],
                             ps_tr[rnd][:].rearrange("p (k f) -> p k f", k=8)),
                         waits=[("tK", tr_cnt[-1])], incs=[("vK", 1)])
                    ev_cnt.append(P.n("vK"))
            lnT_done = P.n("vK")

            mb_groups = [4 if (mb < 8 or 16 <= mb < 24) else 8 for mb in range(48)]
            seq = [(mb, gi) for mb in range(48) for gi in range(mb_groups[mb])]

            slab_cnt = {}
            grp_cnt = []
            qk_ev_cnt, qk_st_cnt = [], []
            v_ev_cnt, v_tr_cnt, v_te_cnt, v_st_cnt = [], [], [], []
            mb_last_grp = {}
            qk_idx_of_group = {}
            v_idx_of_group = {}

            for mb in range(2):
                P.op("gpsimd", lambda g, mb=mb: g.dma_start(wslab[mb % 2][:], wattn[mb]),
                     incs=[("dW", 16)])
                slab_cnt[mb] = P.n("dW")

            nqk = nv = 0
            for j, (mb, gi) in enumerate(seq):
                par = (j % 4) // 2
                half = j % 2
                ps = ps_qkv[par][:, half * 512:half * 512 + 512]
                w = [("dW", slab_cnt[mb])]
                if j == 0:
                    w += [("vK", max(lnT_done, v_lamq2))]
                if j >= 4:
                    pm, _ = seq[j - 4]
                    if pm < 32:
                        w.append(("aK", qk_ev_cnt[qk_idx_of_group[j - 4]]))
                    else:
                        w.append(("vK", v_ev_cnt[v_idx_of_group[j - 4]]))
                pairs = [(wslab[mb % 2][:, k * 128:(k + 1) * 128],
                          lnT_full[:, k * TKV + gi * 512: k * TKV + gi * 512 + 512])
                         for k in range(NKC)]
                P.op("tensor", _mmgroup(ps, pairs), waits=w, incs=[("tK", 1)])
                grp_cnt.append(P.n("tK"))
                if gi == mb_groups[mb] - 1:
                    mb_last_grp[mb] = P.n("tK")
                    if mb + 2 < 48:
                        P.op("gpsimd", lambda g, mb2=mb + 2: g.dma_start(wslab[mb2 % 2][:], wattn[mb2]),
                             waits=[("tK", mb_last_grp[mb])], incs=[("dW", 16)])
                        slab_cnt[mb + 2] = P.n("dW")

                if mb < 32:
                    qk_idx_of_group[j] = nqk
                    w = [("tK", grp_cnt[j])]
                    if nqk >= 4:
                        w.append(("dS", qk_st_cnt[nqk - 4]))
                    if mb < 8:
                        fn = (lambda s, ps=ps, b=qt_sb[nqk % 4]:
                              s.activation(b[:], ps, AF.Copy, scale=0.125))
                    elif 16 <= mb < 24:
                        fn = (lambda s, ps=ps, b=qt_sb[nqk % 4]:
                              s.activation(b[:], ps, AF.Identity, scale=lamq2_col[:]))
                    else:
                        fn = (lambda s, ps=ps, b=qt_sb[nqk % 4]:
                              s.activation(b[:], ps, AF.Copy))
                    P.op("scalar", fn, waits=w, incs=[("aK", 1)])
                    qk_ev_cnt.append(P.n("aK"))
                    P.op("scalar",
                         lambda s, mb=mb, gi=gi, b=qt_sb[nqk % 4]:
                             s.dma_start(qkvT_s[mb, :, gi * 512:(gi + 1) * 512], b[:]),
                         waits=[("aK", qk_ev_cnt[nqk])], incs=[("dS", 16)])
                    qk_st_cnt.append(P.n("dS"))
                    nqk += 1
                else:
                    v_idx_of_group[j] = nv
                    w = [("tK", grp_cnt[j])]
                    if nv >= 2:
                        w.append(("tK", v_tr_cnt[nv - 2]))
                    P.op("vector", lambda v, ps=ps, b=vtmp[nv % 2]: v.tensor_copy(b[:], ps),
                         waits=w, incs=[("vK", 1)])
                    v_ev_cnt.append(P.n("vK"))
                    w = [("vK", v_ev_cnt[nv])]
                    if nv >= 2:
                        w.append(("vK", v_te_cnt[nv - 2]))
                    def vtr(t, b=vtmp[nv % 2], pt=ps_tr[nv % 2]):
                        last = None
                        for tb in range(4):
                            last = t.transpose(pt[:, tb * 128:(tb + 1) * 128],
                                               b[:, tb * 128:(tb + 1) * 128], ident_bf[:])
                        return last
                    P.op("tensor", vtr, waits=w, incs=[("tK", 1)])
                    v_tr_cnt.append(P.n("tK"))
                    w = [("tK", v_tr_cnt[nv])]
                    if nv >= 2:
                        w.append(("dS", v_st_cnt[nv - 2]))
                    P.op("vector",
                         lambda v, b=vt_sb[nv % 2], pt=ps_tr[nv % 2]:
                             v.tensor_copy(b[:].rearrange("p (tb f) -> p tb f", tb=4),
                                           pt[:, 0:512].rearrange("p (tb f) -> p tb f", tb=4)),
                         waits=w, incs=[("vK", 1)])
                    v_te_cnt.append(P.n("vK"))
                    P.op("scalar",
                         lambda s, vb=mb - 32, gi=gi, b=vt_sb[nv % 2]:
                             s.dma_start(v_s2[vb, gi * 4:(gi + 1) * 4, :, :].rearrange("tb p f -> p tb f"),
                                         b[:].rearrange("p (tb f) -> p tb f", tb=4)),
                         waits=[("vK", v_te_cnt[nv])], incs=[("dS", 16)])
                    v_st_cnt.append(P.n("dS"))
                    nv += 1

        if BUILD_PHASES >= 2:
            P.gate()
            _phases_2_to_3(nc, P, E, dict(
                qkvT_s=qkvT_s, v_s2=v_s2, wproj=wproj, x_in=x_in, xmid_s=xmid_s,
                ln2t_s=ln2t_s, ident_bf=ident_bf, ident_f32=ident_f32, ones_bf=ones_bf,
                ln_stats=ln_stats, out_d=out_d, wgu=wgu, wsgu=wsgu, gw_sb=gw_sb,
                wall_in=wall_in, wallbf_s=wallbf_s, act_s=act_s, shift_col=shift_col))

        P.gate(engines=("vector",))
        P.op("vector", lambda v: v.engine_nop())

        P.emit(blk)
    return nc


def _phases_2_to_3(nc, P, E, T):
    qkvT_s, v_s2, wproj, x_in = T["qkvT_s"], T["v_s2"], T["wproj"], T["x_in"]
    xmid_s, ln2t_s = T["xmid_s"], T["ln2t_s"]
    ident_bf, ident_f32, ones_bf = T["ident_bf"], T["ident_f32"], T["ones_bf"]
    ln_stats, out_d = T["ln_stats"], T["out_d"]
    wgu, wsgu, gw_sb = T["wgu"], T["wsgu"], T["gw_sb"]
    wall_in, wallbf_s, act_s = T["wall_in"], T["wallbf_s"], T["act_s"]
    shift_col = T["shift_col"]
    ct_stack = contextlib.ExitStack()
    cT_sb = ct_stack.enter_context(nc.sbuf_tensor("cT_sb", [8, TQ], BF16))
    ct_ev_final = [0]

    with contextlib.ExitStack() as s2x:
        E2x = s2x.enter_context
        attnout = [E2x(nc.sbuf_tensor(f"atno{h}", [128, TQ], BF16)) for h in range(NH)]
        wp_sb = [E2x(nc.sbuf_tensor(f"wp{k}", [128, 2048], BF16)) for k in range(NKC)]

        for k in range(NKC):
            P.op("gpsimd", lambda g, k=k: g.dma_start(wp_sb[k][:], wproj[k]), incs=[("dW", 16)])

        with contextlib.ExitStack() as s2:
            E2 = s2.enter_context
            khat = [E2(nc.sbuf_tensor(f"khat{i}", [128, TKV], BF16)) for i in range(2)]
            qhat = [E2(nc.sbuf_tensor(f"qhat{i}", [128, TQ], BF16)) for i in range(2)]
            v_tm = [E2(nc.sbuf_tensor(f"v_tm{i}", [128, TKV], BF16)) for i in range(2)]
            P_sb = [E2(nc.sbuf_tensor(f"P_sb{i}", [128, 1024], BF16)) for i in range(2)]
            r_sb = E2(nc.sbuf_tensor("r_sb", [128, 1024], F32))
            av_sb = E2(nc.sbuf_tensor("av_sb", [128, 1024], F32))
            den_sb = E2(nc.sbuf_tensor("den_sb", [128, 1024], F32))
            s_ps = [E2(nc.psum_tensor(f"s_ps{i}", [128, 1024], F32)) for i in range(2)]
            av_ps = E2(nc.psum_tensor("av_ps", [128, 1024], F32))
            den_ps = E2(nc.psum_tensor("den_ps", [128, 1024], F32))

            NKT = TKV // 128
            head_load_cnt = {}
            head_Slast = {}
            head_AVlast = {}
            s_hist, e_hist, a_hist = [], [], []
            prev_ev = 0

            def load_head(h):
                par = h % 2
                w = []
                if h >= 2:
                    w = [("tK", head_Slast[h - 2]), ("tK", head_AVlast[h - 2])]
                P.op("sync", lambda sy, h=h, par=par: sy.dma_start(
                    khat[par][0:64, :], qkvT_s[8 + h // 2, (h % 2) * 64:(h % 2) * 64 + 64, :]),
                    waits=w, incs=[("dX", 16)])
                P.op("sync", lambda sy, h=h, par=par: sy.dma_start(
                    khat[par][64:128, :], qkvT_s[24 + h // 2, (h % 2) * 64:(h % 2) * 64 + 64, :]),
                    incs=[("dX", 16)])
                P.op("sync", lambda sy, h=h, par=par: sy.dma_start(
                    qhat[par][0:64, :], qkvT_s[h // 2, (h % 2) * 64:(h % 2) * 64 + 64, 0:TQ]),
                    incs=[("dX", 16)])
                P.op("sync", lambda sy, h=h, par=par: sy.dma_start(
                    qhat[par][64:128, :], qkvT_s[16 + h // 2, (h % 2) * 64:(h % 2) * 64 + 64, 0:TQ]),
                    incs=[("dX", 16)])
                P.op("sync", lambda sy, h=h, par=par: sy.dma_start(
                    v_tm[par][:].rearrange("p (tb f) -> p tb f", tb=32),
                    v_s2[h].rearrange("tb p f -> p tb f")),
                    incs=[("dX", 16)])
                head_load_cnt[h] = P.n("dX")

            load_head(0)
            load_head(1)

            for h in range(NH):
                par = h % 2
                for qt in range(2):
                    pending_avden = None
                    for kt in range(NKT):
                        ns = len(s_hist)
                        spar = kt % 2
                        w = [("dX", head_load_cnt[h])]
                        if ns >= 2:
                            w.append(("aK", e_hist[ns - 2]))
                        def sfn(t, spar=spar, par=par, kt=kt, qt=qt):
                            t.matmul(s_ps[spar][:, 0:512],
                                     khat[par][:, kt * 128:(kt + 1) * 128],
                                     qhat[par][:, qt * 1024:qt * 1024 + 512],
                                     start=True, stop=True)
                            return t.matmul(s_ps[spar][:, 512:1024],
                                            khat[par][:, kt * 128:(kt + 1) * 128],
                                            qhat[par][:, qt * 1024 + 512:qt * 1024 + 1024],
                                            start=True, stop=True)
                        P.op("tensor", sfn, waits=w, incs=[("tK", 1)])
                        s_hist.append(P.n("tK"))
                        if qt == 1 and kt == NKT - 1:
                            head_Slast[h] = P.n("tK")
                        w = [("tK", s_hist[ns])]
                        if ns >= 2:
                            w.append(("tK", a_hist[ns - 2]))
                        P.op("scalar",
                             lambda s, spar=spar, kt=kt: s.activation(
                                 P_sb[kt % 2][:], s_ps[spar][:], AF.Exp, bias=shift_col[:]),
                             waits=w, incs=[("aK", 1)])
                        e_hist.append(P.n("aK"))
                        def avfn(t, par=par, kt=kt, first=(kt == 0), last=(kt == NKT - 1)):
                            t.matmul(av_ps[:, 0:512], v_tm[par][:, kt * 128:(kt + 1) * 128],
                                     P_sb[kt % 2][:, 0:512], start=first, stop=last)
                            t.matmul(av_ps[:, 512:1024], v_tm[par][:, kt * 128:(kt + 1) * 128],
                                     P_sb[kt % 2][:, 512:1024], start=first, stop=last)
                            t.matmul(den_ps[:, 0:512], ones_bf[:], P_sb[kt % 2][:, 0:512],
                                     start=first, stop=last)
                            return t.matmul(den_ps[:, 512:1024], ones_bf[:], P_sb[kt % 2][:, 512:1024],
                                            start=first, stop=last)
                        if pending_avden is not None:
                            pfn, pe_cnt, extra = pending_avden
                            P.op("tensor", pfn, waits=[("aK", pe_cnt)] + extra, incs=[("tK", 1)])
                            a_hist.append(P.n("tK"))
                        extra = [("vK", prev_ev)] if (kt == 0 and prev_ev) else []
                        pending_avden = (avfn, P.n("aK"), extra)
                    pfn, pe_cnt, extra = pending_avden
                    P.op("tensor", pfn, waits=[("aK", pe_cnt)] + extra, incs=[("tK", 1)])
                    a_hist.append(P.n("tK"))
                    if qt == 1:
                        head_AVlast[h] = P.n("tK")
                    P.op("vector", lambda v: v.tensor_copy(av_sb[:], av_ps[:]),
                         waits=[("tK", a_hist[-1])], incs=[("vG", 1)])
                    P.op("vector", lambda v: v.tensor_copy(den_sb[:], den_ps[:]),
                         waits=[("vG", P.n("vG"))], incs=[("vK", 1)])
                    prev_ev = P.n("vK")
                    P.op("vector", lambda v: v.reciprocal(r_sb[:], den_sb[:]),
                         waits=[("vK", prev_ev)], incs=[("vG", 1)])
                    P.op("vector",
                         lambda v, h=h, qt=qt: v.scalar_tensor_tensor(
                             attnout[h][:, qt * 1024:(qt + 1) * 1024], av_sb[:], 0.8, r_sb[:],
                             ALU.mult, ALU.mult),
                         waits=[("vG", P.n("vG"))], incs=[("vK", 1)])
                if h + 2 < NH:
                    load_head(h + 2)

        if BUILD_PHASES >= 3:
            P.gate()
            with contextlib.ExitStack() as s25:
                E25 = s25.enter_context
                attn_tm_t = E25(nc.sbuf_tensor("attn_tm", [128, D], F32))
                attn_tm = [attn_tm_t, attn_tm_t]
                x_tm = [E25(nc.sbuf_tensor(f"x_tm{i}", [128, D], F32)) for i in range(2)]
                xm_tm = [E25(nc.sbuf_tensor(f"xm_tm{i}", [128, D], F32)) for i in range(2)]
                tmp_tm = E25(nc.sbuf_tensor("tmp_tm", [128, D], F32))
                ln2_bf = [E25(nc.sbuf_tensor(f"ln2bf{i}", [128, D], BF16)) for i in range(2)]
                ln2t_st = [E25(nc.sbuf_tensor(f"ln2t_st{i}", [128, D], BF16)) for i in range(2)]
                sc8 = E25(nc.sbuf_tensor("sc8", [128, 8], F32))
                e8 = E25(nc.sbuf_tensor("e8", [128, 8], F32))
                scn = E25(nc.sbuf_tensor("scn", [128, 8], F32))
                ge8 = E25(nc.sbuf_tensor("ge8", [128, 8], F32))
                tmp8 = E25(nc.sbuf_tensor("tmp8", [128, 8], F32))
                comb_bf = E25(nc.sbuf_tensor("comb_bf", [128, 8], BF16))
                g_m1 = E25(nc.sbuf_tensor("g_m1", [128, 1], F32))
                g_nm = E25(nc.sbuf_tensor("g_nm", [128, 1], F32))
                g_s = E25(nc.sbuf_tensor("g_s", [128, 1], F32))
                g_r = E25(nc.sbuf_tensor("g_r", [128, 1], F32))
                g_m2 = E25(nc.sbuf_tensor("g_m2", [128, 1], F32))
                pr_ps = [E25(nc.psum_tensor(f"pr_ps{i}", [128, 1024], F32)) for i in range(2)]
                tr_ps = [E25(nc.psum_tensor(f"tr25_{i}", [128, 1024], BF16)) for i in range(2)]
                g8_ps = E25(nc.psum_tensor("g8_ps", [128, 8], F32))
                ct_ps = E25(nc.psum_tensor("ct_ps", [8, 128], BF16))

                NTS = TQ // 128
                xload, ev_cnt, mm_cnt = [], [], []
                xm_done, xm_store, l2_apply = [], [], []
                l2_tr, l2_ev, l2_st = [], [], []
                gate_mm, ct_ev = [], []

                for ti in range(NTS):
                    par = ti % 2
                    w = [("vK", xm_done[ti - 2])] if ti >= 2 else []
                    P.op("sync", lambda sy, i=ti, par=par: sy.dma_start(
                        x_tm[par][:], x_in[i * 128:(i + 1) * 128, :]), waits=w, incs=[("dX", 16)])
                    xload.append(P.n("dX"))
                    for hc in range(4):
                        ps = pr_ps[hc % 2][:, (hc // 2) * 512:(hc // 2) * 512 + 512]
                        jj = ti * 4 + hc
                        w = []
                        if jj >= 4:
                            w.append(("vK", ev_cnt[jj - 4]))
                        if hc == 0 and ti >= 1:
                            w.append(("aK", prev_stA_sq))
                        pairs = [(attnout[k][:, ti * 128:(ti + 1) * 128],
                                  wp_sb[k][:, hc * 512:(hc + 1) * 512]) for k in range(NKC)]
                        P.op("tensor", _mmgroup(ps, pairs), waits=w, incs=[("tK", 1)])
                        mm_cnt.append(P.n("tK"))
                        P.op("vector",
                             lambda v, par=par, hc=hc, ps=ps: v.tensor_copy(
                                 attn_tm[par][:, hc * 512:(hc + 1) * 512], ps),
                             waits=[("tK", mm_cnt[-1])], incs=[("vK", 1)])
                        ev_cnt.append(P.n("vK"))
                    stA = ln_stats("a", par, attn_tm[par][:], D,
                                   guard_extra=[("vK", ev_cnt[-1])])
                    prev_stA_sq = stA["aK_sq"]
                    P.op("vector",
                         lambda v, par=par, stv=stA: v.tensor_scalar(
                             tmp_tm[:], attn_tm[par][:], stv["mean"][:], stv["rstd"][:],
                             ALU.subtract, ALU.mult),
                         waits=[("aK", stA["aK_rstd"]), ("vK", stA["vK_nmr"])], incs=[("vG", 1)])
                    w = [("vG", P.n("vG")), ("dX", xload[ti])]
                    if ti >= 2:
                        w += [("dS", xm_store[ti - 2]), ("aK", l2_apply[ti - 2])]
                    P.op("vector",
                         lambda v, par=par: v.tensor_tensor(xm_tm[par][:], tmp_tm[:], x_tm[par][:], ALU.add),
                         waits=w, incs=[("vK", 1)])
                    xm_done.append(P.n("vK"))
                    P.op("scalar", lambda s, i=ti, par=par: s.dma_start(
                        xmid_s[i * 128:(i + 1) * 128, :], xm_tm[par][:]),
                        waits=[("vK", xm_done[ti])], incs=[("dS", 16)])
                    xm_store.append(P.n("dS"))
                    stB = ln_stats("b", par, xm_tm[par][:], D,
                                   guard_extra=[("vK", xm_done[ti])])
                    w = [("aK", stB["aK_rstd"]), ("vK", stB["vK_nmr"])]
                    if ti >= 2:
                        w.append(("tK", l2_tr[2 * (ti - 2) + 1]))
                    P.op("scalar",
                         lambda s, par=par, stv=stB: s.activation(
                             ln2_bf[par][:], xm_tm[par][:], AF.Identity,
                             bias=stv["nmr"][:], scale=stv["rstd"][:]),
                         waits=w, incs=[("aK", 1)])
                    l2_apply.append(P.n("aK"))
                    for rnd in range(2):
                        w = [("aK", l2_apply[ti])]
                        prev = 2 * ti + rnd - 2
                        if prev >= 0:
                            w.append(("vK", l2_ev[prev]))
                        def trfn(t, par=par, rnd=rnd):
                            last = None
                            for k in range(8):
                                kc = rnd * 8 + k
                                last = t.transpose(tr_ps[rnd][:, k * 128:(k + 1) * 128],
                                                   ln2_bf[par][:, kc * 128:(kc + 1) * 128], ident_bf[:])
                            return last
                        P.op("tensor", trfn, waits=w, incs=[("tK", 1)])
                        l2_tr.append(P.n("tK"))
                        w = [("tK", l2_tr[-1])]
                        if ti >= 2:
                            w.append(("dS", l2_st[ti - 2]))
                            w.append(("tK", gate_mm[ti - 2]))
                        P.op("vector",
                             lambda v, par=par, rnd=rnd: v.tensor_copy(
                                 ln2t_st[par][:, rnd * 1024:(rnd + 1) * 1024], tr_ps[rnd][:]),
                             waits=w, incs=[("vK", 1)])
                        l2_ev.append(P.n("vK"))
                    P.op("scalar",
                         lambda s, i=ti, par=par: s.dma_start(
                             ln2t_s[:, :, i * 128:(i + 1) * 128].rearrange("k p t -> p k t"),
                             ln2t_st[par][:].rearrange("p (k t) -> p k t", k=NKC)),
                         waits=[("vK", l2_ev[-1])], incs=[("dS", 16)])
                    l2_st.append(P.n("dS"))
                    w = [("vK", l2_ev[-1])]
                    if ti >= 1:
                        w.append(("vK", ct_ev[ti - 1]))
                    pairs = [(ln2t_st[par][:, k * 128:(k + 1) * 128],
                              gw_sb[:, k * 8:(k + 1) * 8]) for k in range(NKC)]
                    P.op("tensor", _mmgroup(g8_ps[:], pairs), waits=w, incs=[("tK", 1)])
                    gate_mm.append(P.n("tK"))

                    def vop(fn, waits=(), inc="vG"):
                        P.op("vector", fn, waits=waits, incs=[(inc, 1)])
                    vop(lambda v: v.tensor_copy(sc8[:], g8_ps[:]), waits=[("tK", gate_mm[-1])])
                    vop(lambda v: v.tensor_reduce(g_m1[:], sc8[:], AX.X, ALU.max),
                        waits=[("vG", P.n("vG"))])
                    vop(lambda v: v.tensor_scalar_mul(g_nm[:], g_m1[:], -1.0),
                        waits=[("vG", P.n("vG"))])
                    v_nm = P.n("vG")
                    P.op("scalar", lambda s: s.activation(e8[:], sc8[:], AF.Exp, bias=g_nm[:]),
                         waits=[("vG", v_nm)], incs=[("aK", 1)])
                    a_e8 = P.n("aK")
                    vop(lambda v: v.tensor_reduce(g_s[:], e8[:], AX.X, ALU.add), waits=[("aK", a_e8)])
                    vop(lambda v: v.reciprocal(g_r[:], g_s[:]), waits=[("vG", P.n("vG"))])
                    vop(lambda v: v.tensor_scalar_mul(scn[:], e8[:], g_r[:]), waits=[("vG", P.n("vG"))])
                    vop(lambda v: v.tensor_reduce(g_m1[:], scn[:], AX.X, ALU.max), waits=[("vG", P.n("vG"))])
                    vop(lambda v: v.tensor_scalar(ge8[:], scn[:], g_m1[:], None, ALU.is_ge),
                        waits=[("vG", P.n("vG"))])
                    vop(lambda v: v.scalar_tensor_tensor(tmp8[:], ge8[:], -1e30, scn[:], ALU.mult, ALU.add),
                        waits=[("vG", P.n("vG"))])
                    vop(lambda v: v.tensor_reduce(g_m2[:], tmp8[:], AX.X, ALU.max), waits=[("vG", P.n("vG"))])
                    vop(lambda v: v.tensor_scalar(ge8[:], scn[:], g_m2[:], None, ALU.is_ge),
                        waits=[("vG", P.n("vG"))])
                    vop(lambda v: v.tensor_tensor(comb_bf[:], ge8[:], scn[:], ALU.mult),
                        waits=[("vG", P.n("vG"))], inc="vK")
                    v_comb = P.n("vK")
                    w = [("vK", v_comb)]
                    if ti >= 1:
                        w.append(("vK", ct_ev[ti - 1]))
                    P.op("tensor", lambda t: t.transpose(ct_ps[:], comb_bf[:], ident_bf[:]),
                         waits=w, incs=[("tK", 1)])
                    t_ct = P.n("tK")
                    P.op("vector", lambda v, i=ti: v.tensor_copy(cT_sb[0:8, i * 128:(i + 1) * 128], ct_ps[:]),
                         waits=[("tK", t_ct)], incs=[("vK", 1)])
                    ct_ev.append(P.n("vK"))
                ct_ev_final[0] = ct_ev[-1]

    if BUILD_PHASES < 4:
        return
    P.gate()
    with contextlib.ExitStack() as s3a:
        E3 = s3a.enter_context
        ln2T = E3(nc.sbuf_tensor("ln2T", [128, NKC * TQ], BF16))
        act_sb_t = E3(nc.sbuf_tensor("act_sb", [128, NFC_E * TQ], BF16))
        act_sb = [act_sb_t, act_sb_t]
        wgsl = [E3(nc.sbuf_tensor(f"wgsl{i}", [128, 2048], BF16)) for i in range(4)]
        c_bc = E3(nc.sbuf_tensor("c_bc", [128, TQ], F32))
        silu_sb = [E3(nc.sbuf_tensor(f"silu{i}", [128, 512], F32)) for i in range(2)]
        t1_sb = [E3(nc.sbuf_tensor(f"t1_{i}", [128, 512], F32)) for i in range(2)]
        crow = E3(nc.sbuf_tensor("crow", [1, TQ], BF16))
        wcast = [E3(nc.sbuf_tensor(f"wcast{i}", [128, 22 * 128], BF16)) for i in range(2)]
        cb_ps = E3(nc.psum_tensor("cb_ps", [128, 512], F32))
        guA = E3(nc.psum_tensor("guA", [128, 1024], F32))
        guB = E3(nc.psum_tensor("guB", [128, 1024], F32))

        NTS = TQ // 128

        P.op("sync", lambda sy: sy.dma_start(
            ln2T[:].rearrange("p (k t) -> p k t", k=NKC),
            ln2t_s.rearrange("k p t -> p k t")), incs=[("dX", 16)])
        d_ln2T = P.n("dX")

        def slab_ap(e, idx):
            if e < NE:
                return wgu[e, idx, :, :]
            half = e - NE
            if idx < 11:
                return wsgu[half * 11 + idx, :, :]
            return wsgu[22 + half * 11 + (idx - 11), :, :]

        def slab_block(e, idx):
            m, which = divmod(idx, 2)
            return slab_ap(e, m if which == 0 else 11 + m)

        pc_loads, pc_stores = [], []
        pc_state = {"n": 0, "pending": None}

        def emit_precast_chunk():
            n = pc_state["n"]
            if pc_state["pending"] is not None:
                hc, q, buf, dcnt = pc_state["pending"]
                P.op("scalar", lambda s, hc=hc, q=q, b=buf: s.dma_start(
                    wallbf_s[hc, q * 22:(q + 1) * 22].rearrange("fc p f -> p fc f"),
                    b[:].rearrange("p (fc f) -> p fc f", fc=22)),
                    waits=[("dW", dcnt)], incs=[("dS", 16)])
                pc_stores.append(P.n("dS"))
                pc_state["pending"] = None
            if n >= 80:
                return
            hc, q = divmod(n, 5)
            w = []
            if n >= 2:
                w.append(("dS", pc_stores[n - 2]))
            buf = wcast[n % 2]
            P.op("gpsimd", lambda g, hc=hc, q=q, b=buf: g.dma_start(
                b[:].rearrange("p (fc f) -> p fc f", fc=22),
                wall_in[hc, q * 22:(q + 1) * 22].rearrange("fc p f -> p fc f")),
                waits=w, incs=[("dW", 16)])
            pc_loads.append(P.n("dW"))
            pc_state["pending"] = (hc, q, buf, P.n("dW"))
            pc_state["n"] = n + 1

        slab_cnt = {}
        slab_used = {}
        act_store = []
        act_last_ev = {}
        jn_hist = []
        cb_prev = 0

        def load_slab(e, idx):
            w = []
            if idx >= 4:
                kp = (e, idx - 4)
            elif e >= 1:
                kp = (e - 1, 18 + ((idx + 2) % 4))
            else:
                kp = None
            if kp is not None and kp in slab_used:
                w.append(("tK", slab_used[kp]))
            P.op("gpsimd", lambda g, ap=slab_block(e, idx), b=wgsl[idx % 4]: g.dma_start(b[:], ap),
                 waits=w, incs=[("dW", 16)])
            slab_cnt[(e, idx)] = P.n("dW")

        for idx in range(4):
            load_slab(0, idx)

        cb_mm_last = 0
        for e in range(N_ELOOP):
            if e < NE:
                w = [("vK", ct_ev_final[0])]
                if cb_mm_last:
                    w.append(("tK", cb_mm_last))
                P.op("sync", lambda sy, e=e: sy.dma_start(crow[:], cT_sb[e:e + 1, :]),
                     waits=w, incs=[("dC", 16)])
                d_crow = P.n("dC")
                cb_done = {}
                for nt in range(4):
                    w = [("dC", d_crow)]
                    if nt >= 1:
                        w.append(("aK", cb_done[nt - 1]))
                    P.op("tensor", lambda t, e=e, nt=nt: t.matmul(
                        cb_ps[:], ones_bf[0:1, :],
                        crow[0:1, nt * 512:(nt + 1) * 512],
                        start=True, stop=True), waits=w, incs=[("tK", 1)])
                    cb_mm_last = P.n("tK")
                    tcb = P.n("tK")
                    w = [("tK", tcb)]
                    if nt == 0 and cb_prev:
                        w.append(("vK", cb_prev))
                    P.op("scalar", lambda s, nt=nt: s.activation(
                        c_bc[:, nt * 512:(nt + 1) * 512], cb_ps[:], AF.Copy),
                        waits=w, incs=[("aK", 1)])
                    cb_done[nt] = P.n("aK")
                cb_ready = cb_done[3]
            else:
                cb_ready = 0

            for m in range(NFC_E):
                i1, i2 = 2 * m, 2 * m + 1
                for nt in range(4):
                    ps1 = guA[:, (nt % 2) * 512:(nt % 2) * 512 + 512]
                    ps2 = guB[:, (nt % 2) * 512:(nt % 2) * 512 + 512]
                    jn = len(jn_hist)
                    w = [("dW", slab_cnt[(e, i1)]), ("dW", slab_cnt[(e, i2)]),
                         ("dX", d_ln2T)]
                    if jn >= 2:
                        w.append(("vK", jn_hist[jn - 2]))
                    def gufn(t, e=e, m=m, nt=nt, ps1=ps1, ps2=ps2):
                        last = None
                        for k in range(NKC):
                            t.matmul(ps1, wgsl[(2 * m) % 4][:, k * 128:(k + 1) * 128],
                                     ln2T[:, k * TQ + nt * 512: k * TQ + (nt + 1) * 512],
                                     start=(k == 0), stop=(k == NKC - 1))
                        for k in range(NKC):
                            last = t.matmul(ps2, wgsl[(2 * m + 1) % 4][:, k * 128:(k + 1) * 128],
                                            ln2T[:, k * TQ + nt * 512: k * TQ + (nt + 1) * 512],
                                            start=(k == 0), stop=(k == NKC - 1))
                        return last
                    P.op("tensor", gufn, waits=w, incs=[("tK", 1)])
                    slab_used[(e, i1)] = slab_used[(e, i2)] = P.n("tK")
                    tmm = P.n("tK")
                    P.op("scalar", lambda s, ps1=ps1, b=silu_sb[nt % 2]: s.activation(b[:], ps1, AF.Silu),
                         waits=[("tK", tmm)], incs=[("aK", 1)])
                    a_silu = P.n("aK")
                    P.op("vector", lambda v, ps2=ps2, b=silu_sb[nt % 2], o=t1_sb[nt % 2]:
                         v.tensor_tensor(o[:], b[:], ps2, ALU.mult),
                         waits=[("aK", a_silu)], incs=[("vG", 1)])
                    dst = act_sb[e % 2][:, m * TQ + nt * 512: m * TQ + (nt + 1) * 512]
                    w = [("vG", P.n("vG"))]
                    if e >= 1 and m == 0 and nt == 0:
                        w.append(("dS", act_store[e - 1]))
                    if e < NE:
                        w.append(("aK", cb_ready))
                        P.op("vector", lambda v, dst=dst, i=t1_sb[nt % 2], nt=nt:
                             v.tensor_tensor(dst, i[:], c_bc[:, nt * 512:(nt + 1) * 512], ALU.mult),
                             waits=w, incs=[("vK", 1)])
                    else:
                        P.op("vector", lambda v, dst=dst, i=t1_sb[nt % 2]: v.tensor_copy(dst, i[:]),
                             waits=w, incs=[("vK", 1)])
                    jn_hist.append(P.n("vK"))
                for nxt in (2 * m + 4, 2 * m + 5):
                    if nxt < 22:
                        load_slab(e, nxt)
                emit_precast_chunk()
            if e + 1 < N_ELOOP:
                for idx in range(4):
                    load_slab(e + 1, idx)
            act_last_ev[e] = P.n("vK")
            cb_prev = P.n("vK")
            rows0 = e * NFC_E
            for tts in range(4):
                P.op("scalar", lambda s, rows0=rows0, b=act_sb[e % 2], tts=tts: s.dma_start(
                    act_s[tts, :, rows0:rows0 + NFC_E, :],
                    b[:].rearrange("p (fc tok) -> p fc tok", fc=NFC_E)[:, :, tts * 512:(tts + 1) * 512]),
                    waits=[("vK", act_last_ev[e])], incs=[("dS", 16)])
            act_store.append(P.n("dS"))
        while pc_state["n"] < 80 or pc_state["pending"] is not None:
            emit_precast_chunk()

    if BUILD_PHASES < 5:
        ct_stack.close()
        return
    ct_stack.close()
    P.gate()
    with contextlib.ExitStack() as s3b:
        E3b = s3b.enter_context
        act_tt = E3b(nc.sbuf_tensor("act_tt", [128, NFC_ALL * 512], BF16))
        wl = [E3b(nc.sbuf_tensor(f"wl{i}", [128, NFC_ALL * 128], BF16)) for i in range(2)]
        ytmp = [E3b(nc.sbuf_tensor(f"ytmp{i}", [128, 512], F32)) for i in range(2)]
        y_tm = E3b(nc.sbuf_tensor("y_tm", [128, 4 * D], BF16))
        xm512 = [E3b(nc.sbuf_tensor(f"xm512_{i}", [128, 512], F32)) for i in range(2)]
        o512 = [E3b(nc.sbuf_tensor(f"o512_{i}", [128, 512], F32)) for i in range(2)]
        tmp512 = E3b(nc.sbuf_tensor("tmp512", [128, 512], F32))
        dn_ps = E3b(nc.psum_tensor("dn_ps", [128, 1024], F32))
        yt_ps = E3b(nc.psum_tensor("yt_ps", [128, 1024], F32))

        act_cnt = []
        wl_cnt, dn_cnt, yev_cnt, ytr_cnt, yte_cnt = {}, {}, {}, {}, {}
        out_cnt = []
        tt_last_mm = []
        add_hist = []
        last_sq = 0

        for tt in range(4):
            w = []
            if tt >= 1:
                w.append(("tK", tt_last_mm[tt - 1]))
                w.append(("dO", out_cnt[-1]))
            P.op("gpsimd", lambda sy, tt=tt: sy.dma_start(
                act_tt[:].rearrange("p (fc t) -> p fc t", fc=NFC_ALL),
                act_s[tt]),
                waits=w, incs=[("dW", 16)])
            act_cnt.append(P.n("dW"))
            for hc in range(16):
                key = (tt, hc)
                pk = (tt, hc - 2) if hc >= 2 else ((tt - 1, hc + 14) if tt >= 1 else None)
                w = []
                if pk in dn_cnt:
                    w.append(("tK", dn_cnt[pk]))
                P.op("sync", lambda g, hc=hc, b=wl[hc % 2]: g.dma_start(
                    b[:].rearrange("p (fc f) -> p fc f", fc=NFC_ALL),
                    wallbf_s[hc].rearrange("fc p f -> p fc f")),
                    waits=w, incs=[("dX", 16)])
                wl_cnt[key] = P.n("dX")
                ps = dn_ps[:, (hc % 2) * 512:(hc % 2) * 512 + 512]
                w = [("dX", wl_cnt[key]), ("dW", act_cnt[tt])]
                if pk in yev_cnt:
                    w.append(("vK", yev_cnt[pk]))
                pairs = [(wl[hc % 2][:, fc * 128:(fc + 1) * 128],
                          act_tt[:, fc * 512:(fc + 1) * 512]) for fc in range(NFC_ALL)]
                P.op("tensor", _mmgroup(ps, pairs), waits=w, incs=[("tK", 1)])
                dn_cnt[key] = P.n("tK")
                if hc == 15:
                    tt_last_mm.append(P.n("tK"))
                w = [("tK", dn_cnt[key])]
                if pk in ytr_cnt:
                    w.append(("tK", ytr_cnt[pk]))
                if hc == 0 and last_sq:
                    w.append(("aK", last_sq))
                P.op("vector", lambda v, ps=ps, b=ytmp[hc % 2]: v.tensor_copy(b[:], ps),
                     waits=w, incs=[("vK", 1)])
                yev_cnt[key] = P.n("vK")
                w = [("vK", yev_cnt[key]), ("gK", 1)]
                if pk in yte_cnt:
                    w.append(("vK", yte_cnt[pk]))
                def ytr(t, hc=hc):
                    last = None
                    for tb in range(4):
                        last = t.transpose(
                            yt_ps[:, (hc % 2) * 512 + tb * 128:(hc % 2) * 512 + (tb + 1) * 128],
                            ytmp[hc % 2][:, tb * 128:(tb + 1) * 128], ident_f32[:])
                    return last
                P.op("tensor", ytr, waits=w, incs=[("tK", 1)])
                ytr_cnt[key] = P.n("tK")
                P.op("vector", lambda v, hc=hc: v.tensor_copy(
                    y_tm[:].rearrange("p (tb f) -> p tb f", tb=4)[:, :, hc * 128:(hc + 1) * 128],
                    yt_ps[:, (hc % 2) * 512:(hc % 2) * 512 + 512].rearrange("p (tb f) -> p tb f", tb=4)),
                    waits=[("tK", ytr_cnt[key])], incs=[("vK", 1)])
                yte_cnt[key] = P.n("vK")
            for tb in range(4):
                ysl = y_tm[:, tb * D:(tb + 1) * D]
                stF = ln_stats("a", tb % 2, ysl, D, guard_extra=[("vK", yte_cnt[(tt, 15)])])
                last_sq = stF["aK_sq"]
                for cc in range(4):
                    rows0 = tt * 512 + tb * 128
                    na = len(add_hist)
                    w = [("vK", add_hist[na - 2])] if na >= 2 else []
                    P.op("sync", lambda sy, rows0=rows0, cc=cc, b=xm512[cc % 2]: sy.dma_start(
                        b[:], xmid_s[rows0:rows0 + 128, cc * 512:(cc + 1) * 512]),
                        waits=w, incs=[("dX", 16)])
                    d_xm = P.n("dX")
                    P.op("vector", lambda v, ysl=ysl, cc=cc, stv=stF: v.tensor_scalar(
                        tmp512[:], ysl[:, cc * 512:(cc + 1) * 512], stv["mean"][:], stv["rstd"][:],
                        ALU.subtract, ALU.mult),
                        waits=[("aK", stF["aK_rstd"]), ("vK", stF["vK_nmr"])], incs=[("vG", 1)])
                    no = len(out_cnt)
                    w = [("vG", P.n("vG")), ("dX", d_xm)]
                    if no >= 2:
                        w.append(("dO", out_cnt[no - 2]))
                    P.op("vector", lambda v, cc=cc, b=xm512[cc % 2], o=o512[cc % 2]:
                         v.tensor_tensor(o[:], tmp512[:], b[:], ALU.add),
                         waits=w, incs=[("vK", 1)])
                    add_hist.append(P.n("vK"))
                    P.op("scalar", lambda s, rows0=rows0, cc=cc, o=o512[cc % 2]: s.dma_start(
                        out_d[rows0:rows0 + 128, cc * 512:(cc + 1) * 512], o[:]),
                        waits=[("vK", add_hist[-1])], incs=[("dO", 16)])
                    out_cnt.append(P.n("dO"))



def _prep_shared(inputs):
    W_attn = np.asarray(inputs["L0_attn_W_attn"], np.float32)
    W_proj = np.asarray(inputs["L0_attn_W_proj"], np.float32)
    gu = np.asarray(inputs["L0_ffn_gate_up_weights"], np.float32)
    dw = np.asarray(inputs["L0_ffn_down_weights"], np.float32)
    sgu = np.asarray(inputs["L0_ffn_shared_gate_up"], np.float32)
    sd = np.asarray(inputs["L0_ffn_shared_down"], np.float32)
    gw = np.asarray(inputs["L0_ffn_gate_weight"], np.float32)

    qk = np.ascontiguousarray(
        W_attn[:, :4096].reshape(16, 128, 32, 128).transpose(2, 1, 0, 3).reshape(32, 128, 2048))
    vv = np.ascontiguousarray(
        W_attn[:, 4096:].reshape(16, 128, 16, 128).transpose(2, 1, 0, 3).reshape(16, 128, 2048))
    wattn = np.concatenate([qk, vv], axis=0)

    wproj = np.ascontiguousarray(W_proj.reshape(16, 128, 2048))

    lam = np.stack([np.asarray(inputs[f"L0_attn_lambda_{n}"], np.float32)
                    for n in ("q1", "k1", "q2", "k2")])
    lam_in = np.ascontiguousarray(lam.reshape(4, 16, 128).transpose(2, 0, 1).reshape(128, 64))

    gw_in = np.ascontiguousarray(gw.T.reshape(16, 128, 8))

    wgu_h = np.ascontiguousarray(
        gu.transpose(0, 2, 1).reshape(NE, 16, 128, 22, 128).transpose(0, 3, 2, 1, 4)
        .reshape(NE, 22, 128, 2048))
    wsgu_h = np.ascontiguousarray(
        sgu.T.reshape(16, 128, 44, 128).transpose(2, 1, 0, 3).reshape(44, 128, 2048))

    wall_t = np.concatenate([dw.transpose(0, 2, 1).reshape(NE * 1408, 2048), sd.T], axis=0)
    wall_h = np.ascontiguousarray(wall_t.reshape(NFC_ALL, 128, 16, 128).transpose(2, 0, 1, 3))

    return dict(wattn=wattn, wproj=wproj, lam_in=lam_in, gw_in=gw_in,
                wgu=wgu_h, wsgu=wsgu_h, wall_in=wall_h)


def make_in_maps(inputs):
    shared = _prep_shared(inputs)
    x = np.asarray(inputs["L0_attn_x"], np.float32)
    in_maps = []
    for c in range(8):
        b, r = divmod(c, 2)
        seq = x[b]
        xc = np.ascontiguousarray(
            np.concatenate([seq[r * TQ:(r + 1) * TQ], seq[(1 - r) * TQ:(2 - r) * TQ]], axis=0))
        m = {"x_in": xc}
        m.update(shared)
        in_maps.append(m)
    return in_maps


def kernel(**inputs):
    in_maps = make_in_maps(inputs)
    nc = build_nc()
    res = run_bass_kernel_spmd(nc, in_maps, core_ids=list(range(8)))
    x = np.asarray(inputs["L0_attn_x"], np.float32)
    out = np.empty_like(x)
    for c in range(8):
        b, r = divmod(c, 2)
        out[b, r * TQ:(r + 1) * TQ, :] = np.asarray(res.results[c]["out"])
    return out


# version 09
# speedup vs baseline: 1.1106x; 1.1042x over previous
import contextlib
import numpy as np
import concourse.bass as bass
import concourse.mybir as mybir
from concourse.bass_utils import run_bass_kernel_spmd

F32 = mybir.dt.float32
BF16 = mybir.dt.bfloat16
AF = mybir.ActivationFunctionType
ALU = mybir.AluOpType
AX = mybir.AxisListType

D = 2048
TKV = 4096
TQ = 2048
NH = 16
NKC = 16
EPS = 1e-5
LAM_INIT = 0.2
EXP_SHIFT = -12.0
NE = 8
NFC_E = 11
NFC_ALL = 110
N_ELOOP = 10

BUILD_PHASES = 5
DEBUG_SCRATCH = False

ENGINES = ("sync", "gpsimd", "scalar", "vector", "tensor")


class Prog:

    def __init__(self, nc, stack):
        self.nc = nc
        self.ops = []
        self.cnt = {}
        self.sems = {}
        self.pending_gate = {e: [] for e in ENGINES}
        self._stack = stack

    def sem(self, name):
        if name not in self.sems:
            self.sems[name] = self._stack.enter_context(self.nc.semaphore(name))
            self.cnt[name] = 0
        return name

    def n(self, name):
        return self.cnt.get(name, 0)

    def op(self, engine, fn, waits=(), incs=()):
        w = list(self.pending_gate[engine]) + [(s, v) for (s, v) in waits if v > 0]
        self.pending_gate[engine] = []
        inc_list = []
        for (s, k) in incs:
            self.sem(s)
            self.cnt[s] += k
            inc_list.append((s, k))
        for (s, _) in w:
            self.sem(s)
        self.ops.append((engine, fn, w, inc_list))

    def gate(self, engines=ENGINES, sems=None):
        sems = sems if sems is not None else list(self.cnt.keys())
        for e in engines:
            for s in sems:
                v = self.cnt.get(s, 0)
                if v > 0:
                    self.pending_gate[e].append((s, v))

    def emit(self, blk):
        prog = self
        for ename in ENGINES:
            def make(ename):
                def section(eng):
                    waited = {}
                    for (en, fn, waits, incs) in prog.ops:
                        if en != ename:
                            continue
                        for (s, v) in waits:
                            if v > waited.get(s, 0):
                                eng.wait_ge(prog.sems[s], v)
                                waited[s] = v
                        inst = fn(eng)
                        for (s, k) in incs:
                            inst.then_inc(prog.sems[s], k)
                return section
            getattr(blk, ename)(make(ename))


def _mmgroup(ps, pairs):
    def fn(t):
        last = None
        for i, (lhs, rhs) in enumerate(pairs):
            last = t.matmul(ps, lhs, rhs, start=(i == 0), stop=(i == len(pairs) - 1))
        return last
    return fn


def build_nc():
    nc = bass.Bass("TRN2", target_bir_lowering=False, debug=False)
    io_kind = "ExternalOutput" if DEBUG_SCRATCH else "Internal"

    x_in = nc.dram_tensor("x_in", [TKV, D], F32, kind="ExternalInput").ap()
    wattn = nc.dram_tensor("wattn", [48, 128, 2048], F32, kind="ExternalInput").ap()
    wproj = nc.dram_tensor("wproj", [16, 128, 2048], F32, kind="ExternalInput").ap()
    lam_in = nc.dram_tensor("lam_in", [128, 64], F32, kind="ExternalInput").ap()
    gw_in = nc.dram_tensor("gw_in", [16, 128, 8], F32, kind="ExternalInput").ap()
    wgu = nc.dram_tensor("wgu", [NE, 22, 128, 2048], F32, kind="ExternalInput").ap()
    wsgu = nc.dram_tensor("wsgu", [44, 128, 2048], F32, kind="ExternalInput").ap()
    wall_in = nc.dram_tensor("wall_in", [16, NFC_ALL, 128, 128], F32, kind="ExternalInput").ap()

    out_d = nc.dram_tensor("out", [TQ, D], F32, kind="ExternalOutput").ap()

    qkvT_s = nc.dram_tensor("qkvT_s", [48, 128, TKV], BF16, kind=io_kind).ap()
    v_s2 = nc.dram_tensor("v_s2", [16, 32, 128, 128], BF16, kind=io_kind).ap()
    xmid_s = nc.dram_tensor("xmid_s", [TQ, D], F32, kind=io_kind).ap()
    ln2t_s = nc.dram_tensor("ln2t_s", [16, 128, TQ], BF16, kind=io_kind).ap()
    act_s = nc.dram_tensor("act_s", [4, 128, NFC_ALL, 512], BF16, kind=io_kind).ap()
    wallbf_s = nc.dram_tensor("wallbf_s", [16, NFC_ALL, 128, 128], BF16, kind="Internal").ap()

    st = contextlib.ExitStack()
    E = st.enter_context
    with st:
        ident_bf = E(nc.sbuf_tensor("ident_bf", [128, 128], BF16))
        ident_f32 = E(nc.sbuf_tensor("ident_f32", [128, 128], F32))
        ones_bf = E(nc.sbuf_tensor("ones_bf", [128, 128], BF16))
        lam_sb = E(nc.sbuf_tensor("lam_sb", [128, 64], F32))
        prod2 = E(nc.sbuf_tensor("prod2", [128, 32], F32))
        rowsum2f = E(nc.sbuf_tensor("rowsum2f", [128, 2], F32))
        rowsum2 = E(nc.sbuf_tensor("rowsum2", [128, 2], BF16))
        expd = E(nc.sbuf_tensor("expd", [1, 2], F32))
        lam_val = E(nc.sbuf_tensor("lam_val", [1, 1], F32))
        lam_val_bf = E(nc.sbuf_tensor("lam_val_bf", [1, 1], BF16))
        lamq2_col = E(nc.sbuf_tensor("lamq2_col", [128, 1], F32))
        gw_sb = E(nc.sbuf_tensor("gw_sb", [128, 128], BF16))
        sq_scrap = E(nc.sbuf_tensor("sq_scrap", [128, 2048], F32))
        eps_col = E(nc.sbuf_tensor("eps_col", [128, 1], F32))
        shift_col = E(nc.sbuf_tensor("shift_col", [128, 1], F32))
        COLS = {}
        for nm in ("sum", "ssq", "mean", "m2", "var", "rstd", "nmr"):
            for grp in ("a", "b"):
                COLS[(grp, nm)] = [E(nc.sbuf_tensor(f"c_{grp}{nm}{i}", [128, 1], F32))
                                   for i in range(2)]

        P = Prog(nc, st)
        for s in ("dX", "dW", "dS", "dO", "dC", "tK", "vK", "aK", "gK", "vG", "aG"):
            P.sem(s)

        blk = E(nc.Block())

        ln_last_var = {}

        def ln_stats(grp, par, x_ap, n_free, guard_extra=()):
            c = lambda nm: COLS[(grp, nm)][par]
            war = [("vK", ln_last_var.get((grp, par), 0))]
            P.op("scalar",
                 lambda s, x=x_ap, o=c("ssq"): s.activation(
                     sq_scrap[:, 0:n_free], x, AF.Square, accum_out=o[:]),
                 waits=list(guard_extra) + war, incs=[("aK", 1)])
            a_ssq = P.n("aK")
            P.op("vector",
                 lambda v, x=x_ap, o=c("sum"): v.tensor_reduce(o[:], x, AX.X, ALU.add),
                 waits=guard_extra, incs=[("vG", 1)])
            P.op("vector",
                 lambda v, i=c("sum"), o=c("mean"): v.tensor_scalar_mul(o[:], i[:], 1.0 / n_free),
                 waits=[("vG", P.n("vG"))], incs=[("vG", 1)])
            P.op("vector",
                 lambda v, i=c("mean"), o=c("m2"): v.tensor_tensor(o[:], i[:], i[:], ALU.mult),
                 waits=[("vG", P.n("vG"))], incs=[("vG", 1)])
            P.op("vector",
                 lambda v, i=c("ssq"), m=c("m2"), o=c("var"):
                     v.scalar_tensor_tensor(o[:], i[:], 1.0 / n_free, m[:], ALU.mult, ALU.subtract),
                 waits=[("vG", P.n("vG")), ("aK", a_ssq)], incs=[("vK", 1)])
            v_var = P.n("vK")
            ln_last_var[(grp, par)] = v_var
            P.op("scalar",
                 lambda s, i=c("var"), o=c("rstd"): s.activation(o[:], i[:], AF.Ln, bias=eps_col[:]),
                 waits=[("vK", v_var)], incs=[("aG", 1)])
            P.op("scalar",
                 lambda s, io=c("rstd"): s.activation(io[:], io[:], AF.Exp, scale=-0.5),
                 waits=[("aG", P.n("aG"))], incs=[("aK", 1)])
            a_rstd = P.n("aK")
            P.op("vector",
                 lambda v, m=c("mean"), r=c("rstd"), o=c("nmr"):
                     v.scalar_tensor_tensor(o[:], m[:], -1.0, r[:], ALU.mult, ALU.mult),
                 waits=[("aK", a_rstd)], incs=[("vK", 1)])
            return {"aK_sq": a_ssq, "aK_rstd": a_rstd, "vK_nmr": P.n("vK"),
                    "mean": c("mean"), "rstd": c("rstd"), "nmr": c("nmr")}

        P.op("gpsimd", lambda g: g.memset(ident_bf[:], 0.0), incs=[("gK", 1)])
        P.op("gpsimd", lambda g: g.affine_select(out=ident_bf[:], in_=ident_bf[:],
                                                 compare_op=ALU.not_equal, fill=1.0, base=0,
                                                 pattern=[[-1, 128]], channel_multiplier=1),
             waits=[("gK", P.n("gK"))])
        P.op("gpsimd", lambda g: g.memset(ident_f32[:], 0.0), incs=[("gK", 1)])
        P.op("gpsimd", lambda g: g.affine_select(out=ident_f32[:], in_=ident_f32[:],
                                                 compare_op=ALU.not_equal, fill=1.0, base=0,
                                                 pattern=[[-1, 128]], channel_multiplier=1),
             waits=[("gK", P.n("gK"))])
        P.op("gpsimd", lambda g: g.memset(eps_col[:], EPS))
        P.op("gpsimd", lambda g: g.memset(shift_col[:], EXP_SHIFT))
        P.op("gpsimd", lambda g: g.memset(ones_bf[:], 1.0), incs=[("gK", 1)])
        g_const = P.n("gK")
        P.op("gpsimd", lambda g: g.dma_start(lam_sb[:], lam_in), incs=[("dW", 16)])
        P.op("gpsimd", lambda g: g.dma_start(gw_sb[:].rearrange("p (a e) -> p a e", a=16),
                                             gw_in.rearrange("a p e -> p a e")), incs=[("dW", 16)])
        d_lamgw = P.n("dW")

        ph0_ps = contextlib.ExitStack()
        lam_ps = ph0_ps.enter_context(nc.psum_tensor("lam_ps", [1, 2], F32))
        lam_bc_ps = ph0_ps.enter_context(nc.psum_tensor("lam_bc_ps", [128, 1], F32))

        P.op("vector", lambda v: v.tensor_tensor(prod2[:, 0:16], lam_sb[:, 0:16], lam_sb[:, 16:32], ALU.mult),
             waits=[("dW", d_lamgw)])
        P.op("vector", lambda v: v.tensor_tensor(prod2[:, 16:32], lam_sb[:, 32:48], lam_sb[:, 48:64], ALU.mult),
             incs=[("vG", 1)])
        P.op("vector", lambda v: v.tensor_reduce(rowsum2f[:, 0:1], prod2[:, 0:16], AX.X, ALU.add),
             waits=[("vG", P.n("vG"))])
        P.op("vector", lambda v: v.tensor_reduce(rowsum2f[:, 1:2], prod2[:, 16:32], AX.X, ALU.add),
             incs=[("vG", 1)])
        P.op("vector", lambda v: v.tensor_copy(rowsum2[:], rowsum2f[:]),
             waits=[("vG", P.n("vG"))], incs=[("vK", 1)])
        P.op("tensor", lambda t: t.matmul(lam_ps[:], ones_bf[:, 0:1], rowsum2[:], start=True, stop=True),
             waits=[("gK", g_const), ("vK", P.n("vK"))], incs=[("tK", 1)])
        P.op("scalar", lambda s: s.activation(expd[:], lam_ps[:], AF.Exp),
             waits=[("tK", P.n("tK"))], incs=[("aK", 1)])
        P.op("vector", lambda v: v.tensor_tensor(lam_val[:], expd[:, 0:1], expd[:, 1:2], ALU.subtract),
             waits=[("aK", P.n("aK"))], incs=[("vG", 1)])
        P.op("vector", lambda v: v.tensor_scalar_add(lam_val[:], lam_val[:], LAM_INIT),
             waits=[("vG", P.n("vG"))], incs=[("vG", 1)])
        P.op("vector", lambda v: v.tensor_copy(lam_val_bf[:], lam_val[:]),
             waits=[("vG", P.n("vG"))], incs=[("vK", 1)])
        P.op("tensor", lambda t: t.matmul(lam_bc_ps[:], ones_bf[0:1, :], lam_val_bf[:], start=True, stop=True),
             waits=[("vK", P.n("vK"))], incs=[("tK", 1)])
        P.op("vector", lambda v: v.tensor_scalar_mul(lamq2_col[:], lam_bc_ps[:], -0.125),
             waits=[("tK", P.n("tK"))], incs=[("vK", 1)])
        v_lamq2 = P.n("vK")
        ph0_ps.close()

        with contextlib.ExitStack() as ph1:
            E1 = ph1.enter_context
            lnT_full = E1(nc.sbuf_tensor("lnT_full", [128, NKC * TKV], BF16))
            xa = [E1(nc.sbuf_tensor(f"xa{i}", [128, D], F32)) for i in range(2)]
            ln_tm = [E1(nc.sbuf_tensor(f"ln_tm{i}", [128, D], BF16)) for i in range(2)]
            wslab = [E1(nc.sbuf_tensor(f"wslab{i}", [128, 2048], BF16)) for i in range(2)]
            qt_sb = [E1(nc.sbuf_tensor(f"qt_sb{i}", [128, 512], BF16)) for i in range(4)]
            vtmp = [E1(nc.sbuf_tensor(f"vtmp{i}", [128, 512], BF16)) for i in range(2)]
            vt_sb = [E1(nc.sbuf_tensor(f"vt_sb{i}", [128, 512], BF16)) for i in range(2)]
            ps_qkv = [E1(nc.psum_tensor(f"ps_qkv{i}", [128, 1024], F32)) for i in range(2)]
            ps_tr = [E1(nc.psum_tensor(f"ps_tr{i}", [128, 1024], BF16)) for i in range(2)]

            NT_A = TKV // 128
            lnT_r = lnT_full[:].rearrange("p (k t) -> p k t", k=NKC)

            apply_cnt, tr_cnt, ev_cnt, load_cnt = [], [], [], []
            for i in range(NT_A):
                par = i % 2
                w = [("aK", apply_cnt[i - 2])] if i >= 2 else []
                P.op("sync", lambda sy, i=i, par=par: sy.dma_start(
                    xa[par][:], x_in[i * 128:(i + 1) * 128, :]), waits=w, incs=[("dX", 16)])
                load_cnt.append(P.n("dX"))
                st_i = ln_stats("a", par, xa[par][:], D, guard_extra=[("dX", load_cnt[i])])
                w = [("vK", st_i["vK_nmr"])]
                if i >= 2:
                    w.append(("tK", tr_cnt[2 * (i - 2) + 1]))
                P.op("scalar",
                     lambda s, par=par, stv=st_i: s.activation(
                         ln_tm[par][:], xa[par][:], AF.Identity, bias=stv["nmr"][:], scale=stv["rstd"][:]),
                     waits=w, incs=[("aK", 1)])
                apply_cnt.append(P.n("aK"))
                for rnd in range(2):
                    w = [("aK", apply_cnt[i]), ("gK", g_const)]
                    if i == 0:
                        w.append(("vK", v_lamq2))
                    prev = 2 * i + rnd - 2
                    if prev >= 0:
                        w.append(("vK", ev_cnt[prev]))
                    def trfn(t, par=par, rnd=rnd):
                        last = None
                        for k in range(8):
                            kc = rnd * 8 + k
                            last = t.transpose(ps_tr[rnd][:, k * 128:(k + 1) * 128],
                                               ln_tm[par][:, kc * 128:(kc + 1) * 128], ident_bf[:])
                        return last
                    P.op("tensor", trfn, waits=w, incs=[("tK", 1)])
                    tr_cnt.append(P.n("tK"))
                    P.op("vector",
                         lambda v, i=i, rnd=rnd: v.tensor_copy(
                             lnT_r[:, rnd * 8:(rnd + 1) * 8, i * 128:(i + 1) * 128],
                             ps_tr[rnd][:].rearrange("p (k f) -> p k f", k=8)),
                         waits=[("tK", tr_cnt[-1])], incs=[("vK", 1)])
                    ev_cnt.append(P.n("vK"))
            lnT_done = P.n("vK")

            mb_groups = [4 if (mb < 8 or 16 <= mb < 24) else 8 for mb in range(48)]
            mb_order = list(range(0, 8)) + list(range(16, 24)) + \
                list(range(8, 16)) + list(range(24, 48))
            seq = [(mb, gi) for mb in mb_order for gi in range(mb_groups[mb])]
            lnT_q_done = ev_cnt[31]

            slab_cnt = {}
            grp_cnt = []
            qk_ev_cnt, qk_st_cnt = [], []
            v_ev_cnt, v_tr_cnt, v_te_cnt, v_st_cnt = [], [], [], []
            mb_last_grp = {}
            qk_idx_of_group = {}
            v_idx_of_group = {}

            slab_buf = {mb: oi % 2 for oi, mb in enumerate(mb_order)}
            for oi in range(2):
                mb0 = mb_order[oi]
                P.op("gpsimd", lambda g, mb0=mb0, b=wslab[oi % 2]: g.dma_start(b[:], wattn[mb0]),
                     incs=[("dW", 16)])
                slab_cnt[mb0] = P.n("dW")

            nqk = nv = 0
            for j, (mb, gi) in enumerate(seq):
                par = (j % 4) // 2
                half = j % 2
                ps = ps_qkv[par][:, half * 512:half * 512 + 512]
                w = [("dW", slab_cnt[mb])]
                if j == 0:
                    w += [("vK", max(lnT_q_done, v_lamq2))]
                if gi == 0 and not (mb < 8 or 16 <= mb < 24):
                    w += [("vK", lnT_done)]
                if j >= 4:
                    pm, _ = seq[j - 4]
                    if pm < 32:
                        w.append(("aK", qk_ev_cnt[qk_idx_of_group[j - 4]]))
                    else:
                        w.append(("vK", v_ev_cnt[v_idx_of_group[j - 4]]))
                pairs = [(wslab[slab_buf[mb]][:, k * 128:(k + 1) * 128],
                          lnT_full[:, k * TKV + gi * 512: k * TKV + gi * 512 + 512])
                         for k in range(NKC)]
                P.op("tensor", _mmgroup(ps, pairs), waits=w, incs=[("tK", 1)])
                grp_cnt.append(P.n("tK"))
                if gi == mb_groups[mb] - 1:
                    mb_last_grp[mb] = P.n("tK")
                    oi = mb_order.index(mb)
                    if oi + 2 < 48:
                        mb2 = mb_order[oi + 2]
                        P.op("gpsimd", lambda g, mb2=mb2, b=wslab[(oi + 2) % 2]: g.dma_start(b[:], wattn[mb2]),
                             waits=[("tK", mb_last_grp[mb])], incs=[("dW", 16)])
                        slab_cnt[mb2] = P.n("dW")

                if mb < 32:
                    qk_idx_of_group[j] = nqk
                    w = [("tK", grp_cnt[j])]
                    if nqk >= 4:
                        w.append(("dS", qk_st_cnt[nqk - 4]))
                    if mb < 8:
                        fn = (lambda s, ps=ps, b=qt_sb[nqk % 4]:
                              s.activation(b[:], ps, AF.Copy, scale=0.125))
                    elif 16 <= mb < 24:
                        fn = (lambda s, ps=ps, b=qt_sb[nqk % 4]:
                              s.activation(b[:], ps, AF.Identity, scale=lamq2_col[:]))
                    else:
                        fn = (lambda s, ps=ps, b=qt_sb[nqk % 4]:
                              s.activation(b[:], ps, AF.Copy))
                    P.op("scalar", fn, waits=w, incs=[("aK", 1)])
                    qk_ev_cnt.append(P.n("aK"))
                    P.op("scalar",
                         lambda s, mb=mb, gi=gi, b=qt_sb[nqk % 4]:
                             s.dma_start(qkvT_s[mb, :, gi * 512:(gi + 1) * 512], b[:]),
                         waits=[("aK", qk_ev_cnt[nqk])], incs=[("dS", 16)])
                    qk_st_cnt.append(P.n("dS"))
                    nqk += 1
                else:
                    v_idx_of_group[j] = nv
                    w = [("tK", grp_cnt[j])]
                    if nv >= 2:
                        w.append(("tK", v_tr_cnt[nv - 2]))
                    P.op("vector", lambda v, ps=ps, b=vtmp[nv % 2]: v.tensor_copy(b[:], ps),
                         waits=w, incs=[("vK", 1)])
                    v_ev_cnt.append(P.n("vK"))
                    w = [("vK", v_ev_cnt[nv])]
                    if nv >= 2:
                        w.append(("vK", v_te_cnt[nv - 2]))
                    def vtr(t, b=vtmp[nv % 2], pt=ps_tr[nv % 2]):
                        last = None
                        for tb in range(4):
                            last = t.transpose(pt[:, tb * 128:(tb + 1) * 128],
                                               b[:, tb * 128:(tb + 1) * 128], ident_bf[:])
                        return last
                    P.op("tensor", vtr, waits=w, incs=[("tK", 1)])
                    v_tr_cnt.append(P.n("tK"))
                    w = [("tK", v_tr_cnt[nv])]
                    if nv >= 2:
                        w.append(("dS", v_st_cnt[nv - 2]))
                    P.op("vector",
                         lambda v, b=vt_sb[nv % 2], pt=ps_tr[nv % 2]:
                             v.tensor_copy(b[:].rearrange("p (tb f) -> p tb f", tb=4),
                                           pt[:, 0:512].rearrange("p (tb f) -> p tb f", tb=4)),
                         waits=w, incs=[("vK", 1)])
                    v_te_cnt.append(P.n("vK"))
                    P.op("scalar",
                         lambda s, vb=mb - 32, gi=gi, b=vt_sb[nv % 2]:
                             s.dma_start(v_s2[vb, gi * 4:(gi + 1) * 4, :, :].rearrange("tb p f -> p tb f"),
                                         b[:].rearrange("p (tb f) -> p tb f", tb=4)),
                         waits=[("vK", v_te_cnt[nv])], incs=[("dS", 16)])
                    v_st_cnt.append(P.n("dS"))
                    nv += 1

        if BUILD_PHASES >= 2:
            P.gate()
            _phases_2_to_3(nc, P, E, dict(
                qkvT_s=qkvT_s, v_s2=v_s2, wproj=wproj, x_in=x_in, xmid_s=xmid_s,
                ln2t_s=ln2t_s, ident_bf=ident_bf, ident_f32=ident_f32, ones_bf=ones_bf,
                ln_stats=ln_stats, out_d=out_d, wgu=wgu, wsgu=wsgu, gw_sb=gw_sb,
                wall_in=wall_in, wallbf_s=wallbf_s, act_s=act_s, shift_col=shift_col))

        P.gate(engines=("vector",))
        P.op("vector", lambda v: v.engine_nop())

        P.emit(blk)
    return nc


def _phases_2_to_3(nc, P, E, T):
    qkvT_s, v_s2, wproj, x_in = T["qkvT_s"], T["v_s2"], T["wproj"], T["x_in"]
    xmid_s, ln2t_s = T["xmid_s"], T["ln2t_s"]
    ident_bf, ident_f32, ones_bf = T["ident_bf"], T["ident_f32"], T["ones_bf"]
    ln_stats, out_d = T["ln_stats"], T["out_d"]
    wgu, wsgu, gw_sb = T["wgu"], T["wsgu"], T["gw_sb"]
    wall_in, wallbf_s, act_s = T["wall_in"], T["wallbf_s"], T["act_s"]
    shift_col = T["shift_col"]
    ct_stack = contextlib.ExitStack()
    cT_sb = ct_stack.enter_context(nc.sbuf_tensor("cT_sb", [8, TQ], BF16))
    ct_ev_final = [0]

    with contextlib.ExitStack() as s2x:
        E2x = s2x.enter_context
        attnout = [E2x(nc.sbuf_tensor(f"atno{h}", [128, TQ], BF16)) for h in range(NH)]
        wp_sb = [E2x(nc.sbuf_tensor(f"wp{k}", [128, 2048], BF16)) for k in range(NKC)]

        for k in range(NKC):
            P.op("gpsimd", lambda g, k=k: g.dma_start(wp_sb[k][:], wproj[k]), incs=[("dW", 16)])

        with contextlib.ExitStack() as s2:
            E2 = s2.enter_context
            khat = [E2(nc.sbuf_tensor(f"khat{i}", [128, TKV], BF16)) for i in range(2)]
            qhat = [E2(nc.sbuf_tensor(f"qhat{i}", [128, TQ], BF16)) for i in range(2)]
            v_tm = [E2(nc.sbuf_tensor(f"v_tm{i}", [128, TKV], BF16)) for i in range(2)]
            P_sb = [E2(nc.sbuf_tensor(f"P_sb{i}", [128, 1024], BF16)) for i in range(4)]
            Pp_sb = [E2(nc.sbuf_tensor(f"Pp_sb{i}", [128, 1024], BF16)) for i in range(2)]
            r_sb = E2(nc.sbuf_tensor("r_sb", [128, 1024], F32))
            av_sb = E2(nc.sbuf_tensor("av_sb", [128, 1024], F32))
            den_sb = E2(nc.sbuf_tensor("den_sb", [128, 1024], F32))
            s_ps = [E2(nc.psum_tensor(f"s_ps{i}", [128, 1024], F32)) for i in range(2)]
            av_ps = E2(nc.psum_tensor("av_ps", [128, 1024], F32))
            den_ps = E2(nc.psum_tensor("den_ps", [128, 1024], F32))

            NKT = TKV // 128
            head_load_cnt = {}
            head_Slast = {}
            head_AVlast = {}
            s_hist, e_hist, a_hist = [], [], []
            prev_ev = 0
            prev_pair_last = 0
            prev_den_last = 0

            def load_head(h):
                par = h % 2
                w = []
                if h >= 2:
                    w = [("tK", head_Slast[h - 2]), ("tK", head_AVlast[h - 2])]
                P.op("sync", lambda sy, h=h, par=par: sy.dma_start(
                    khat[par][0:64, :], qkvT_s[8 + h // 2, (h % 2) * 64:(h % 2) * 64 + 64, :]),
                    waits=w, incs=[("dX", 16)])
                P.op("sync", lambda sy, h=h, par=par: sy.dma_start(
                    khat[par][64:128, :], qkvT_s[24 + h // 2, (h % 2) * 64:(h % 2) * 64 + 64, :]),
                    incs=[("dX", 16)])
                P.op("sync", lambda sy, h=h, par=par: sy.dma_start(
                    qhat[par][0:64, :], qkvT_s[h // 2, (h % 2) * 64:(h % 2) * 64 + 64, 0:TQ]),
                    incs=[("dX", 16)])
                P.op("sync", lambda sy, h=h, par=par: sy.dma_start(
                    qhat[par][64:128, :], qkvT_s[16 + h // 2, (h % 2) * 64:(h % 2) * 64 + 64, 0:TQ]),
                    incs=[("dX", 16)])
                P.op("sync", lambda sy, h=h, par=par: sy.dma_start(
                    v_tm[par][:].rearrange("p (tb f) -> p tb f", tb=32),
                    v_s2[h].rearrange("tb p f -> p tb f")),
                    incs=[("dX", 16)])
                head_load_cnt[h] = P.n("dX")

            load_head(0)
            load_head(1)

            for h in range(NH):
                par = h % 2
                for qt in range(2):
                    pending_avden = None
                    pending_den = None
                    pair_cnt = {}
                    den_cnt = {}
                    for kt in range(NKT):
                        ns = len(s_hist)
                        spar = kt % 2
                        w = [("dX", head_load_cnt[h])]
                        if ns >= 2:
                            w.append(("aK", e_hist[ns - 2]))
                        def sfn(t, spar=spar, par=par, kt=kt, qt=qt):
                            t.matmul(s_ps[spar][:, 0:512],
                                     khat[par][:, kt * 128:(kt + 1) * 128],
                                     qhat[par][:, qt * 1024:qt * 1024 + 512],
                                     start=True, stop=True)
                            return t.matmul(s_ps[spar][:, 512:1024],
                                            khat[par][:, kt * 128:(kt + 1) * 128],
                                            qhat[par][:, qt * 1024 + 512:qt * 1024 + 1024],
                                            start=True, stop=True)
                        P.op("tensor", sfn, waits=w, incs=[("tK", 1)])
                        s_hist.append(P.n("tK"))
                        if qt == 1 and kt == NKT - 1:
                            head_Slast[h] = P.n("tK")
                        w = [("tK", s_hist[ns])]
                        if ns >= 4:
                            w.append(("tK", a_hist[ns - 4]))
                            pp = (kt - 4) // 2
                            if pp in pair_cnt:
                                w.append(("vK", pair_cnt[pp]))
                            elif prev_pair_last:
                                w.append(("vK", prev_pair_last))
                        P.op("scalar",
                             lambda s, spar=spar, kt=kt: s.activation(
                                 P_sb[kt % 4][:], s_ps[spar][:], AF.Exp, bias=shift_col[:]),
                             waits=w, incs=[("aK", 1)])
                        e_hist.append(P.n("aK"))
                        def avfn(t, par=par, kt=kt, first=(kt == 0), last=(kt == NKT - 1)):
                            t.matmul(av_ps[:, 0:512], v_tm[par][:, kt * 128:(kt + 1) * 128],
                                     P_sb[kt % 4][:, 0:512], start=first, stop=last)
                            return t.matmul(av_ps[:, 512:1024], v_tm[par][:, kt * 128:(kt + 1) * 128],
                                            P_sb[kt % 4][:, 512:1024], start=first, stop=last)
                        if pending_avden is not None:
                            pfn, pe_cnt, extra = pending_avden
                            P.op("tensor", pfn, waits=[("aK", pe_cnt)] + extra, incs=[("tK", 1)])
                            a_hist.append(P.n("tK"))
                        extra = [("vK", prev_ev)] if (kt == 0 and prev_ev) else []
                        pending_avden = (avfn, P.n("aK"), extra)
                        if kt % 2 == 1:
                            p = kt // 2
                            w = [("aK", e_hist[ns])]
                            if p >= 2 and (p - 2) in den_cnt:
                                w.append(("tK", den_cnt[p - 2]))
                            elif p < 2 and prev_den_last:
                                w.append(("tK", prev_den_last))
                            P.op("vector", lambda v, p=p, kt=kt: v.tensor_tensor(
                                Pp_sb[p % 2][:], P_sb[(kt - 1) % 4][:], P_sb[kt % 4][:], ALU.add),
                                waits=w, incs=[("vK", 1)])
                            pair_cnt[p] = P.n("vK")
                            def mkden(p):
                                def denfn(t, p=p, first=(p == 0), last=(p == NKT // 2 - 1)):
                                    t.matmul(den_ps[:, 0:512], ones_bf[:], Pp_sb[p % 2][:, 0:512],
                                             start=first, stop=last)
                                    return t.matmul(den_ps[:, 512:1024], ones_bf[:],
                                                    Pp_sb[p % 2][:, 512:1024],
                                                    start=first, stop=last)
                                return denfn
                            if pending_den is not None:
                                dfn, dp = pending_den
                                dw_ = [("vK", pair_cnt[dp])]
                                if dp == 0 and prev_ev:
                                    dw_.append(("vK", prev_ev))
                                P.op("tensor", dfn, waits=dw_, incs=[("tK", 1)])
                                den_cnt[dp] = P.n("tK")
                            pending_den = (mkden(p), p)
                    pfn, pe_cnt, extra = pending_avden
                    P.op("tensor", pfn, waits=[("aK", pe_cnt)] + extra, incs=[("tK", 1)])
                    a_hist.append(P.n("tK"))
                    if qt == 1:
                        head_AVlast[h] = P.n("tK")
                    dfn, dp = pending_den
                    P.op("tensor", dfn, waits=[("vK", pair_cnt[dp])], incs=[("tK", 1)])
                    den_cnt[dp] = P.n("tK")
                    prev_pair_last = pair_cnt[NKT // 2 - 1]
                    prev_den_last = den_cnt[NKT // 2 - 1]
                    P.op("vector", lambda v: v.tensor_copy(av_sb[:], av_ps[:]),
                         waits=[("tK", a_hist[-1]), ("tK", den_cnt[NKT // 2 - 1])], incs=[("vG", 1)])
                    P.op("vector", lambda v: v.tensor_copy(den_sb[:], den_ps[:]),
                         waits=[("vG", P.n("vG"))], incs=[("vK", 1)])
                    prev_ev = P.n("vK")
                    P.op("vector", lambda v: v.reciprocal(r_sb[:], den_sb[:]),
                         waits=[("vK", prev_ev)], incs=[("vG", 1)])
                    P.op("vector",
                         lambda v, h=h, qt=qt: v.scalar_tensor_tensor(
                             attnout[h][:, qt * 1024:(qt + 1) * 1024], av_sb[:], 0.8, r_sb[:],
                             ALU.mult, ALU.mult),
                         waits=[("vG", P.n("vG"))], incs=[("vK", 1)])
                if h + 2 < NH:
                    load_head(h + 2)

        if BUILD_PHASES >= 3:
            P.gate()
            with contextlib.ExitStack() as s25:
                E25 = s25.enter_context
                attn_tm_t = E25(nc.sbuf_tensor("attn_tm", [128, D], F32))
                attn_tm = [attn_tm_t, attn_tm_t]
                x_tm = [E25(nc.sbuf_tensor(f"x_tm{i}", [128, D], F32)) for i in range(2)]
                xm_tm = [E25(nc.sbuf_tensor(f"xm_tm{i}", [128, D], F32)) for i in range(2)]
                tmp_tm = E25(nc.sbuf_tensor("tmp_tm", [128, D], F32))
                ln2_bf = [E25(nc.sbuf_tensor(f"ln2bf{i}", [128, D], BF16)) for i in range(2)]
                ln2t_st = [E25(nc.sbuf_tensor(f"ln2t_st{i}", [128, D], BF16)) for i in range(2)]
                sc8 = E25(nc.sbuf_tensor("sc8", [128, 8], F32))
                e8 = E25(nc.sbuf_tensor("e8", [128, 8], F32))
                scn = E25(nc.sbuf_tensor("scn", [128, 8], F32))
                ge8 = E25(nc.sbuf_tensor("ge8", [128, 8], F32))
                tmp8 = E25(nc.sbuf_tensor("tmp8", [128, 8], F32))
                comb_bf = E25(nc.sbuf_tensor("comb_bf", [128, 8], BF16))
                g_m1 = E25(nc.sbuf_tensor("g_m1", [128, 1], F32))
                g_nm = E25(nc.sbuf_tensor("g_nm", [128, 1], F32))
                g_s = E25(nc.sbuf_tensor("g_s", [128, 1], F32))
                g_r = E25(nc.sbuf_tensor("g_r", [128, 1], F32))
                g_m2 = E25(nc.sbuf_tensor("g_m2", [128, 1], F32))
                pr_ps = [E25(nc.psum_tensor(f"pr_ps{i}", [128, 1024], F32)) for i in range(2)]
                tr_ps = [E25(nc.psum_tensor(f"tr25_{i}", [128, 1024], BF16)) for i in range(2)]
                g8_ps = E25(nc.psum_tensor("g8_ps", [128, 8], F32))
                ct_ps = E25(nc.psum_tensor("ct_ps", [8, 128], BF16))

                NTS = TQ // 128
                xload, ev_cnt, mm_cnt = [], [], []
                xm_done, xm_store, l2_apply = [], [], []
                l2_tr, l2_ev, l2_st = [], [], []
                gate_mm, ct_ev = [], []

                for ti in range(NTS):
                    par = ti % 2
                    w = [("vK", xm_done[ti - 2])] if ti >= 2 else []
                    P.op("sync", lambda sy, i=ti, par=par: sy.dma_start(
                        x_tm[par][:], x_in[i * 128:(i + 1) * 128, :]), waits=w, incs=[("dX", 16)])
                    xload.append(P.n("dX"))
                    for hc in range(4):
                        ps = pr_ps[hc % 2][:, (hc // 2) * 512:(hc // 2) * 512 + 512]
                        jj = ti * 4 + hc
                        w = []
                        if jj >= 4:
                            w.append(("vK", ev_cnt[jj - 4]))
                        if hc == 0 and ti >= 1:
                            w.append(("aK", prev_stA_sq))
                        pairs = [(attnout[k][:, ti * 128:(ti + 1) * 128],
                                  wp_sb[k][:, hc * 512:(hc + 1) * 512]) for k in range(NKC)]
                        P.op("tensor", _mmgroup(ps, pairs), waits=w, incs=[("tK", 1)])
                        mm_cnt.append(P.n("tK"))
                        P.op("vector",
                             lambda v, par=par, hc=hc, ps=ps: v.tensor_copy(
                                 attn_tm[par][:, hc * 512:(hc + 1) * 512], ps),
                             waits=[("tK", mm_cnt[-1])], incs=[("vK", 1)])
                        ev_cnt.append(P.n("vK"))
                    stA = ln_stats("a", par, attn_tm[par][:], D,
                                   guard_extra=[("vK", ev_cnt[-1])])
                    prev_stA_sq = stA["aK_sq"]
                    P.op("vector",
                         lambda v, par=par, stv=stA: v.tensor_scalar(
                             tmp_tm[:], attn_tm[par][:], stv["mean"][:], stv["rstd"][:],
                             ALU.subtract, ALU.mult),
                         waits=[("aK", stA["aK_rstd"]), ("vK", stA["vK_nmr"])], incs=[("vG", 1)])
                    w = [("vG", P.n("vG")), ("dX", xload[ti])]
                    if ti >= 2:
                        w += [("dS", xm_store[ti - 2]), ("aK", l2_apply[ti - 2])]
                    P.op("vector",
                         lambda v, par=par: v.tensor_tensor(xm_tm[par][:], tmp_tm[:], x_tm[par][:], ALU.add),
                         waits=w, incs=[("vK", 1)])
                    xm_done.append(P.n("vK"))
                    P.op("scalar", lambda s, i=ti, par=par: s.dma_start(
                        xmid_s[i * 128:(i + 1) * 128, :], xm_tm[par][:]),
                        waits=[("vK", xm_done[ti])], incs=[("dS", 16)])
                    xm_store.append(P.n("dS"))
                    stB = ln_stats("b", par, xm_tm[par][:], D,
                                   guard_extra=[("vK", xm_done[ti])])
                    w = [("aK", stB["aK_rstd"]), ("vK", stB["vK_nmr"])]
                    if ti >= 2:
                        w.append(("tK", l2_tr[2 * (ti - 2) + 1]))
                    P.op("scalar",
                         lambda s, par=par, stv=stB: s.activation(
                             ln2_bf[par][:], xm_tm[par][:], AF.Identity,
                             bias=stv["nmr"][:], scale=stv["rstd"][:]),
                         waits=w, incs=[("aK", 1)])
                    l2_apply.append(P.n("aK"))
                    for rnd in range(2):
                        w = [("aK", l2_apply[ti])]
                        prev = 2 * ti + rnd - 2
                        if prev >= 0:
                            w.append(("vK", l2_ev[prev]))
                        def trfn(t, par=par, rnd=rnd):
                            last = None
                            for k in range(8):
                                kc = rnd * 8 + k
                                last = t.transpose(tr_ps[rnd][:, k * 128:(k + 1) * 128],
                                                   ln2_bf[par][:, kc * 128:(kc + 1) * 128], ident_bf[:])
                            return last
                        P.op("tensor", trfn, waits=w, incs=[("tK", 1)])
                        l2_tr.append(P.n("tK"))
                        w = [("tK", l2_tr[-1])]
                        if ti >= 2:
                            w.append(("dS", l2_st[ti - 2]))
                            w.append(("tK", gate_mm[ti - 2]))
                        P.op("vector",
                             lambda v, par=par, rnd=rnd: v.tensor_copy(
                                 ln2t_st[par][:, rnd * 1024:(rnd + 1) * 1024], tr_ps[rnd][:]),
                             waits=w, incs=[("vK", 1)])
                        l2_ev.append(P.n("vK"))
                    P.op("scalar",
                         lambda s, i=ti, par=par: s.dma_start(
                             ln2t_s[:, :, i * 128:(i + 1) * 128].rearrange("k p t -> p k t"),
                             ln2t_st[par][:].rearrange("p (k t) -> p k t", k=NKC)),
                         waits=[("vK", l2_ev[-1])], incs=[("dS", 16)])
                    l2_st.append(P.n("dS"))
                    w = [("vK", l2_ev[-1])]
                    if ti >= 1:
                        w.append(("vK", ct_ev[ti - 1]))
                    pairs = [(ln2t_st[par][:, k * 128:(k + 1) * 128],
                              gw_sb[:, k * 8:(k + 1) * 8]) for k in range(NKC)]
                    P.op("tensor", _mmgroup(g8_ps[:], pairs), waits=w, incs=[("tK", 1)])
                    gate_mm.append(P.n("tK"))

                    def vop(fn, waits=(), inc="vG"):
                        P.op("vector", fn, waits=waits, incs=[(inc, 1)])
                    vop(lambda v: v.tensor_copy(sc8[:], g8_ps[:]), waits=[("tK", gate_mm[-1])])
                    vop(lambda v: v.tensor_reduce(g_m1[:], sc8[:], AX.X, ALU.max),
                        waits=[("vG", P.n("vG"))])
                    vop(lambda v: v.tensor_scalar_mul(g_nm[:], g_m1[:], -1.0),
                        waits=[("vG", P.n("vG"))])
                    v_nm = P.n("vG")
                    P.op("scalar", lambda s: s.activation(e8[:], sc8[:], AF.Exp, bias=g_nm[:]),
                         waits=[("vG", v_nm)], incs=[("aK", 1)])
                    a_e8 = P.n("aK")
                    vop(lambda v: v.tensor_reduce(g_s[:], e8[:], AX.X, ALU.add), waits=[("aK", a_e8)])
                    vop(lambda v: v.reciprocal(g_r[:], g_s[:]), waits=[("vG", P.n("vG"))])
                    vop(lambda v: v.tensor_scalar_mul(scn[:], e8[:], g_r[:]), waits=[("vG", P.n("vG"))])
                    vop(lambda v: v.tensor_reduce(g_m1[:], scn[:], AX.X, ALU.max), waits=[("vG", P.n("vG"))])
                    vop(lambda v: v.tensor_scalar(ge8[:], scn[:], g_m1[:], None, ALU.is_ge),
                        waits=[("vG", P.n("vG"))])
                    vop(lambda v: v.scalar_tensor_tensor(tmp8[:], ge8[:], -1e30, scn[:], ALU.mult, ALU.add),
                        waits=[("vG", P.n("vG"))])
                    vop(lambda v: v.tensor_reduce(g_m2[:], tmp8[:], AX.X, ALU.max), waits=[("vG", P.n("vG"))])
                    vop(lambda v: v.tensor_scalar(ge8[:], scn[:], g_m2[:], None, ALU.is_ge),
                        waits=[("vG", P.n("vG"))])
                    vop(lambda v: v.tensor_tensor(comb_bf[:], ge8[:], scn[:], ALU.mult),
                        waits=[("vG", P.n("vG"))], inc="vK")
                    v_comb = P.n("vK")
                    w = [("vK", v_comb)]
                    if ti >= 1:
                        w.append(("vK", ct_ev[ti - 1]))
                    P.op("tensor", lambda t: t.transpose(ct_ps[:], comb_bf[:], ident_bf[:]),
                         waits=w, incs=[("tK", 1)])
                    t_ct = P.n("tK")
                    P.op("vector", lambda v, i=ti: v.tensor_copy(cT_sb[0:8, i * 128:(i + 1) * 128], ct_ps[:]),
                         waits=[("tK", t_ct)], incs=[("vK", 1)])
                    ct_ev.append(P.n("vK"))
                ct_ev_final[0] = ct_ev[-1]

    if BUILD_PHASES < 4:
        return
    P.gate()
    with contextlib.ExitStack() as s3a:
        E3 = s3a.enter_context
        ln2T = E3(nc.sbuf_tensor("ln2T", [128, NKC * TQ], BF16))
        act_sb_t = E3(nc.sbuf_tensor("act_sb", [128, NFC_E * TQ], BF16))
        act_sb = [act_sb_t, act_sb_t]
        wgsl = [E3(nc.sbuf_tensor(f"wgsl{i}", [128, 2048], BF16)) for i in range(4)]
        c_bc = E3(nc.sbuf_tensor("c_bc", [128, TQ], F32))
        silu_sb = [E3(nc.sbuf_tensor(f"silu{i}", [128, 512], F32)) for i in range(2)]
        t1_sb = [E3(nc.sbuf_tensor(f"t1_{i}", [128, 512], F32)) for i in range(2)]
        crow = E3(nc.sbuf_tensor("crow", [1, TQ], BF16))
        wcast = [E3(nc.sbuf_tensor(f"wcast{i}", [128, 22 * 128], BF16)) for i in range(2)]
        cb_ps = E3(nc.psum_tensor("cb_ps", [128, 512], F32))
        guA = E3(nc.psum_tensor("guA", [128, 1024], F32))
        guB = E3(nc.psum_tensor("guB", [128, 1024], F32))

        NTS = TQ // 128

        P.op("sync", lambda sy: sy.dma_start(
            ln2T[:].rearrange("p (k t) -> p k t", k=NKC),
            ln2t_s.rearrange("k p t -> p k t")), incs=[("dX", 16)])
        d_ln2T = P.n("dX")

        def slab_ap(e, idx):
            if e < NE:
                return wgu[e, idx, :, :]
            half = e - NE
            if idx < 11:
                return wsgu[half * 11 + idx, :, :]
            return wsgu[22 + half * 11 + (idx - 11), :, :]

        def slab_block(e, idx):
            m, which = divmod(idx, 2)
            return slab_ap(e, m if which == 0 else 11 + m)

        pc_loads, pc_stores = [], []
        pc_state = {"n": 0, "pending": None}

        def emit_precast_chunk():
            n = pc_state["n"]
            if pc_state["pending"] is not None:
                hc, q, buf, dcnt = pc_state["pending"]
                P.op("scalar", lambda s, hc=hc, q=q, b=buf: s.dma_start(
                    wallbf_s[hc, q * 22:(q + 1) * 22].rearrange("fc p f -> p fc f"),
                    b[:].rearrange("p (fc f) -> p fc f", fc=22)),
                    waits=[("dW", dcnt)], incs=[("dS", 16)])
                pc_stores.append(P.n("dS"))
                pc_state["pending"] = None
            if n >= 80:
                return
            hc, q = divmod(n, 5)
            w = []
            if n >= 2:
                w.append(("dS", pc_stores[n - 2]))
            buf = wcast[n % 2]
            P.op("gpsimd", lambda g, hc=hc, q=q, b=buf: g.dma_start(
                b[:].rearrange("p (fc f) -> p fc f", fc=22),
                wall_in[hc, q * 22:(q + 1) * 22].rearrange("fc p f -> p fc f")),
                waits=w, incs=[("dW", 16)])
            pc_loads.append(P.n("dW"))
            pc_state["pending"] = (hc, q, buf, P.n("dW"))
            pc_state["n"] = n + 1

        slab_cnt = {}
        slab_used = {}
        act_store = []
        act_last_ev = {}
        jn_hist = []
        cb_prev = 0

        def load_slab(e, idx):
            w = []
            if idx >= 4:
                kp = (e, idx - 4)
            elif e >= 1:
                kp = (e - 1, 18 + ((idx + 2) % 4))
            else:
                kp = None
            if kp is not None and kp in slab_used:
                w.append(("tK", slab_used[kp]))
            P.op("gpsimd", lambda g, ap=slab_block(e, idx), b=wgsl[idx % 4]: g.dma_start(b[:], ap),
                 waits=w, incs=[("dW", 16)])
            slab_cnt[(e, idx)] = P.n("dW")

        for idx in range(4):
            load_slab(0, idx)

        cb_mm_last = 0
        for e in range(N_ELOOP):
            if e < NE:
                w = [("vK", ct_ev_final[0])]
                if cb_mm_last:
                    w.append(("tK", cb_mm_last))
                P.op("sync", lambda sy, e=e: sy.dma_start(crow[:], cT_sb[e:e + 1, :]),
                     waits=w, incs=[("dC", 16)])
                d_crow = P.n("dC")
                cb_done = {}
                for nt in range(4):
                    w = [("dC", d_crow)]
                    if nt >= 1:
                        w.append(("aK", cb_done[nt - 1]))
                    P.op("tensor", lambda t, e=e, nt=nt: t.matmul(
                        cb_ps[:], ones_bf[0:1, :],
                        crow[0:1, nt * 512:(nt + 1) * 512],
                        start=True, stop=True), waits=w, incs=[("tK", 1)])
                    cb_mm_last = P.n("tK")
                    tcb = P.n("tK")
                    w = [("tK", tcb)]
                    if nt == 0 and cb_prev:
                        w.append(("vK", cb_prev))
                    P.op("scalar", lambda s, nt=nt: s.activation(
                        c_bc[:, nt * 512:(nt + 1) * 512], cb_ps[:], AF.Copy),
                        waits=w, incs=[("aK", 1)])
                    cb_done[nt] = P.n("aK")
                cb_ready = cb_done[3]
            else:
                cb_ready = 0

            for m in range(NFC_E):
                i1, i2 = 2 * m, 2 * m + 1
                for nt in range(4):
                    ps1 = guA[:, (nt % 2) * 512:(nt % 2) * 512 + 512]
                    ps2 = guB[:, (nt % 2) * 512:(nt % 2) * 512 + 512]
                    jn = len(jn_hist)
                    w = [("dW", slab_cnt[(e, i1)]), ("dW", slab_cnt[(e, i2)]),
                         ("dX", d_ln2T)]
                    if jn >= 2:
                        w.append(("vK", jn_hist[jn - 2]))
                    def gufn(t, e=e, m=m, nt=nt, ps1=ps1, ps2=ps2):
                        last = None
                        for k in range(NKC):
                            t.matmul(ps1, wgsl[(2 * m) % 4][:, k * 128:(k + 1) * 128],
                                     ln2T[:, k * TQ + nt * 512: k * TQ + (nt + 1) * 512],
                                     start=(k == 0), stop=(k == NKC - 1))
                        for k in range(NKC):
                            last = t.matmul(ps2, wgsl[(2 * m + 1) % 4][:, k * 128:(k + 1) * 128],
                                            ln2T[:, k * TQ + nt * 512: k * TQ + (nt + 1) * 512],
                                            start=(k == 0), stop=(k == NKC - 1))
                        return last
                    P.op("tensor", gufn, waits=w, incs=[("tK", 1)])
                    slab_used[(e, i1)] = slab_used[(e, i2)] = P.n("tK")
                    tmm = P.n("tK")
                    P.op("scalar", lambda s, ps1=ps1, b=silu_sb[nt % 2]: s.activation(b[:], ps1, AF.Silu),
                         waits=[("tK", tmm)], incs=[("aK", 1)])
                    a_silu = P.n("aK")
                    P.op("vector", lambda v, ps2=ps2, b=silu_sb[nt % 2], o=t1_sb[nt % 2]:
                         v.tensor_tensor(o[:], b[:], ps2, ALU.mult),
                         waits=[("aK", a_silu)], incs=[("vG", 1)])
                    dst = act_sb[e % 2][:, m * TQ + nt * 512: m * TQ + (nt + 1) * 512]
                    w = [("vG", P.n("vG"))]
                    if e >= 1 and m == 0 and nt == 0:
                        w.append(("dS", act_store[e - 1]))
                    if e < NE:
                        w.append(("aK", cb_ready))
                        P.op("vector", lambda v, dst=dst, i=t1_sb[nt % 2], nt=nt:
                             v.tensor_tensor(dst, i[:], c_bc[:, nt * 512:(nt + 1) * 512], ALU.mult),
                             waits=w, incs=[("vK", 1)])
                    else:
                        P.op("vector", lambda v, dst=dst, i=t1_sb[nt % 2]: v.tensor_copy(dst, i[:]),
                             waits=w, incs=[("vK", 1)])
                    jn_hist.append(P.n("vK"))
                for nxt in (2 * m + 4, 2 * m + 5):
                    if nxt < 22:
                        load_slab(e, nxt)
                emit_precast_chunk()
            if e + 1 < N_ELOOP:
                for idx in range(4):
                    load_slab(e + 1, idx)
            act_last_ev[e] = P.n("vK")
            cb_prev = P.n("vK")
            rows0 = e * NFC_E
            for tts in range(4):
                P.op("scalar", lambda s, rows0=rows0, b=act_sb[e % 2], tts=tts: s.dma_start(
                    act_s[tts, :, rows0:rows0 + NFC_E, :],
                    b[:].rearrange("p (fc tok) -> p fc tok", fc=NFC_E)[:, :, tts * 512:(tts + 1) * 512]),
                    waits=[("vK", act_last_ev[e])], incs=[("dS", 16)])
            act_store.append(P.n("dS"))
        while pc_state["n"] < 80 or pc_state["pending"] is not None:
            emit_precast_chunk()

    if BUILD_PHASES < 5:
        ct_stack.close()
        return
    ct_stack.close()
    P.gate()
    with contextlib.ExitStack() as s3b:
        E3b = s3b.enter_context
        act_tt = E3b(nc.sbuf_tensor("act_tt", [128, NFC_ALL * 512], BF16))
        wl = [E3b(nc.sbuf_tensor(f"wl{i}", [128, NFC_ALL * 128], BF16)) for i in range(2)]
        ytmp = [E3b(nc.sbuf_tensor(f"ytmp{i}", [128, 512], F32)) for i in range(2)]
        y_tm = E3b(nc.sbuf_tensor("y_tm", [128, 4 * D], BF16))
        xm512 = [E3b(nc.sbuf_tensor(f"xm512_{i}", [128, 512], F32)) for i in range(2)]
        o512 = [E3b(nc.sbuf_tensor(f"o512_{i}", [128, 512], F32)) for i in range(2)]
        tmp512 = E3b(nc.sbuf_tensor("tmp512", [128, 512], F32))
        dn_ps = E3b(nc.psum_tensor("dn_ps", [128, 1024], F32))
        yt_ps = E3b(nc.psum_tensor("yt_ps", [128, 1024], F32))

        act_cnt = []
        wl_cnt, dn_cnt, yev_cnt, ytr_cnt, yte_cnt = {}, {}, {}, {}, {}
        out_cnt = []
        tt_last_mm = []
        add_hist = []
        last_sq = 0

        for tt in range(4):
            w = []
            if tt >= 1:
                w.append(("tK", tt_last_mm[tt - 1]))
                w.append(("dO", out_cnt[-1]))
            P.op("gpsimd", lambda sy, tt=tt: sy.dma_start(
                act_tt[:].rearrange("p (fc t) -> p fc t", fc=NFC_ALL),
                act_s[tt]),
                waits=w, incs=[("dW", 16)])
            act_cnt.append(P.n("dW"))
            for hc in range(16):
                key = (tt, hc)
                pk = (tt, hc - 2) if hc >= 2 else ((tt - 1, hc + 14) if tt >= 1 else None)
                w = []
                if pk in dn_cnt:
                    w.append(("tK", dn_cnt[pk]))
                P.op("sync", lambda g, hc=hc, b=wl[hc % 2]: g.dma_start(
                    b[:].rearrange("p (fc f) -> p fc f", fc=NFC_ALL),
                    wallbf_s[hc].rearrange("fc p f -> p fc f")),
                    waits=w, incs=[("dX", 16)])
                wl_cnt[key] = P.n("dX")
                ps = dn_ps[:, (hc % 2) * 512:(hc % 2) * 512 + 512]
                w = [("dX", wl_cnt[key]), ("dW", act_cnt[tt])]
                if pk in yev_cnt:
                    w.append(("vK", yev_cnt[pk]))
                pairs = [(wl[hc % 2][:, fc * 128:(fc + 1) * 128],
                          act_tt[:, fc * 512:(fc + 1) * 512]) for fc in range(NFC_ALL)]
                P.op("tensor", _mmgroup(ps, pairs), waits=w, incs=[("tK", 1)])
                dn_cnt[key] = P.n("tK")
                if hc == 15:
                    tt_last_mm.append(P.n("tK"))
                w = [("tK", dn_cnt[key])]
                if pk in ytr_cnt:
                    w.append(("tK", ytr_cnt[pk]))
                if hc == 0 and last_sq:
                    w.append(("aK", last_sq))
                P.op("vector", lambda v, ps=ps, b=ytmp[hc % 2]: v.tensor_copy(b[:], ps),
                     waits=w, incs=[("vK", 1)])
                yev_cnt[key] = P.n("vK")
                w = [("vK", yev_cnt[key]), ("gK", 1)]
                if pk in yte_cnt:
                    w.append(("vK", yte_cnt[pk]))
                def ytr(t, hc=hc):
                    last = None
                    for tb in range(4):
                        last = t.transpose(
                            yt_ps[:, (hc % 2) * 512 + tb * 128:(hc % 2) * 512 + (tb + 1) * 128],
                            ytmp[hc % 2][:, tb * 128:(tb + 1) * 128], ident_f32[:])
                    return last
                P.op("tensor", ytr, waits=w, incs=[("tK", 1)])
                ytr_cnt[key] = P.n("tK")
                P.op("vector", lambda v, hc=hc: v.tensor_copy(
                    y_tm[:].rearrange("p (tb f) -> p tb f", tb=4)[:, :, hc * 128:(hc + 1) * 128],
                    yt_ps[:, (hc % 2) * 512:(hc % 2) * 512 + 512].rearrange("p (tb f) -> p tb f", tb=4)),
                    waits=[("tK", ytr_cnt[key])], incs=[("vK", 1)])
                yte_cnt[key] = P.n("vK")
            for tb in range(4):
                ysl = y_tm[:, tb * D:(tb + 1) * D]
                stF = ln_stats("a", tb % 2, ysl, D, guard_extra=[("vK", yte_cnt[(tt, 15)])])
                last_sq = stF["aK_sq"]
                for cc in range(4):
                    rows0 = tt * 512 + tb * 128
                    na = len(add_hist)
                    w = [("vK", add_hist[na - 2])] if na >= 2 else []
                    P.op("sync", lambda sy, rows0=rows0, cc=cc, b=xm512[cc % 2]: sy.dma_start(
                        b[:], xmid_s[rows0:rows0 + 128, cc * 512:(cc + 1) * 512]),
                        waits=w, incs=[("dX", 16)])
                    d_xm = P.n("dX")
                    P.op("vector", lambda v, ysl=ysl, cc=cc, stv=stF: v.tensor_scalar(
                        tmp512[:], ysl[:, cc * 512:(cc + 1) * 512], stv["mean"][:], stv["rstd"][:],
                        ALU.subtract, ALU.mult),
                        waits=[("aK", stF["aK_rstd"]), ("vK", stF["vK_nmr"])], incs=[("vG", 1)])
                    no = len(out_cnt)
                    w = [("vG", P.n("vG")), ("dX", d_xm)]
                    if no >= 2:
                        w.append(("dO", out_cnt[no - 2]))
                    P.op("vector", lambda v, cc=cc, b=xm512[cc % 2], o=o512[cc % 2]:
                         v.tensor_tensor(o[:], tmp512[:], b[:], ALU.add),
                         waits=w, incs=[("vK", 1)])
                    add_hist.append(P.n("vK"))
                    P.op("scalar", lambda s, rows0=rows0, cc=cc, o=o512[cc % 2]: s.dma_start(
                        out_d[rows0:rows0 + 128, cc * 512:(cc + 1) * 512], o[:]),
                        waits=[("vK", add_hist[-1])], incs=[("dO", 16)])
                    out_cnt.append(P.n("dO"))



def _prep_shared(inputs):
    W_attn = np.asarray(inputs["L0_attn_W_attn"], np.float32)
    W_proj = np.asarray(inputs["L0_attn_W_proj"], np.float32)
    gu = np.asarray(inputs["L0_ffn_gate_up_weights"], np.float32)
    dw = np.asarray(inputs["L0_ffn_down_weights"], np.float32)
    sgu = np.asarray(inputs["L0_ffn_shared_gate_up"], np.float32)
    sd = np.asarray(inputs["L0_ffn_shared_down"], np.float32)
    gw = np.asarray(inputs["L0_ffn_gate_weight"], np.float32)

    qk = np.ascontiguousarray(
        W_attn[:, :4096].reshape(16, 128, 32, 128).transpose(2, 1, 0, 3).reshape(32, 128, 2048))
    vv = np.ascontiguousarray(
        W_attn[:, 4096:].reshape(16, 128, 16, 128).transpose(2, 1, 0, 3).reshape(16, 128, 2048))
    wattn = np.concatenate([qk, vv], axis=0)

    wproj = np.ascontiguousarray(W_proj.reshape(16, 128, 2048))

    lam = np.stack([np.asarray(inputs[f"L0_attn_lambda_{n}"], np.float32)
                    for n in ("q1", "k1", "q2", "k2")])
    lam_in = np.ascontiguousarray(lam.reshape(4, 16, 128).transpose(2, 0, 1).reshape(128, 64))

    gw_in = np.ascontiguousarray(gw.T.reshape(16, 128, 8))

    wgu_h = np.ascontiguousarray(
        gu.transpose(0, 2, 1).reshape(NE, 16, 128, 22, 128).transpose(0, 3, 2, 1, 4)
        .reshape(NE, 22, 128, 2048))
    wsgu_h = np.ascontiguousarray(
        sgu.T.reshape(16, 128, 44, 128).transpose(2, 1, 0, 3).reshape(44, 128, 2048))

    wall_t = np.concatenate([dw.transpose(0, 2, 1).reshape(NE * 1408, 2048), sd.T], axis=0)
    wall_h = np.ascontiguousarray(wall_t.reshape(NFC_ALL, 128, 16, 128).transpose(2, 0, 1, 3))

    return dict(wattn=wattn, wproj=wproj, lam_in=lam_in, gw_in=gw_in,
                wgu=wgu_h, wsgu=wsgu_h, wall_in=wall_h)


def make_in_maps(inputs):
    shared = _prep_shared(inputs)
    x = np.asarray(inputs["L0_attn_x"], np.float32)
    in_maps = []
    for c in range(8):
        b, r = divmod(c, 2)
        seq = x[b]
        xc = np.ascontiguousarray(
            np.concatenate([seq[r * TQ:(r + 1) * TQ], seq[(1 - r) * TQ:(2 - r) * TQ]], axis=0))
        m = {"x_in": xc}
        m.update(shared)
        in_maps.append(m)
    return in_maps


def kernel(**inputs):
    in_maps = make_in_maps(inputs)
    nc = build_nc()
    res = run_bass_kernel_spmd(nc, in_maps, core_ids=list(range(8)))
    x = np.asarray(inputs["L0_attn_x"], np.float32)
    out = np.empty_like(x)
    for c in range(8):
        b, r = divmod(c, 2)
        out[b, r * TQ:(r + 1) * TQ, :] = np.asarray(res.results[c]["out"])
    return out
